# version 16
# baseline (speedup 1.0000x reference)
import numpy as np
import ml_dtypes
from contextlib import ExitStack
import concourse.bass as bass
import concourse.mybir as mybir
from concourse.bass_utils import run_bass_kernel_spmd

F32 = mybir.dt.float32
BF16 = mybir.dt.bfloat16
FP8 = mybir.dt.float8e4
U32 = mybir.dt.uint32
AF = mybir.ActivationFunctionType
ALU = mybir.AluOpType
AX = mybir.AxisListType

NCORE = 8
B, S, D, H = 2, 8192, 1024, 4096
NTOK = B * S
TPC = NTOK // NCORE
NT = TPC // 128
DCH = D // 128
HCH = H // 32 // 4
HCH = H // 128
HH = H // 2
CH = 256
NCHUNK = NTOK // CH
NYS = 1
YR = NTOK // NYS
CPC = NCHUNK // NCORE
EPS_LN = 1e-5
EPS_ABS = 1e-6

XFM_R0 = 0
GT_R0 = 2048
PKG_R = GT_R0 + CPC * 3 * 16


class Plan:
    ENGINES = ("tensor", "vector", "scalar", "sync", "gpsimd")

    def __init__(self):
        self.streams = {e: [] for e in self.ENGINES}
        self.sems = {}

    def sem(self, name):
        if name not in self.sems:
            self.sems[name] = [None, 0]

    def cnt(self, name):
        return self.sems[name][1]

    def wait(self, eng, semname, val):
        if val is None or val <= 0:
            return
        assert val <= self.sems[semname][1], (
            f"{eng} waits {semname}>={val}, planned {self.sems[semname][1]}")
        s = self.sems[semname]
        self.streams[eng].append(lambda e, regs: e.wait_ge(s[0], val))

    def op(self, eng, fn, inc=None, incby=1):
        if inc is None:
            self.streams[eng].append(lambda e, regs: fn(e))
            return None
        s = self.sems[inc]
        s[1] += incby
        self.streams[eng].append(lambda e, regs: fn(e).then_inc(s[0], incby))
        return s[1]

    def opr(self, eng, fn, inc, incby=1):
        s = self.sems[inc]
        s[1] += incby
        self.streams[eng].append(
            lambda e, regs: fn(e, regs).then_inc(s[0], incby))
        return s[1]

    def selfw(self, eng):
        sem = {"tensor": "sT", "vector": "sV", "scalar": "sA"}[eng]
        self.wait(eng, sem, self.cnt(sem))

    def dma(self, eng, out_ap, in_ap, inc):
        return self.op(eng, lambda e: e.dma_start(out=out_ap, in_=in_ap),
                       inc=inc, incby=16)


def build_program(debug=False):
    nc = bass.Bass("TRN2", target_bir_lowering=False)
    dr = {}

    def din(name, shape, dt=F32):
        dr[name] = nc.dram_tensor(name, shape, dt, kind="ExternalInput")

    def dint(name, shape, dt=F32):
        kind = "ExternalOutput" if debug else "Internal"
        dr[name] = nc.dram_tensor(name, shape, dt, kind=kind)

    din("xr", [TPC, D])
    din("wfcT", [D, H])
    din("wpT", [H, D])
    din("w0T", [D, 128])
    din("wvT", [D, 128])
    din("bvbc", [128, 128])
    din("wbc", [128, H])
    din("bcol", [128, HCH])
    din("wcol", [128, HCH])
    din("eta", [1, 1])
    din("idf", [128, 128])
    din("idb", [128, 128], BF16)
    din("maskl", [128, 128])
    din("masku", [128, 128])
    din("consts", [128, 4])
    din("yoff", [NT, 1], U32)
    din("wslab", [256, H])

    dint("xmid", [TPC, D])
    dr["gin"] = nc.dram_tensor("gin", [1, 2], F32)
    dr["gout"] = nc.dram_tensor("gout_d", [1, 2], F32)
    dr["wqin"] = nc.dram_tensor("wqin", [256, H], FP8)
    dr["wqout"] = nc.dram_tensor("wqout", [NCORE * 256, H], FP8)
    dr["agin"] = nc.dram_tensor("agin", [PKG_R, D], F32)
    dr["agout"] = nc.dram_tensor("agout", [NCORE * PKG_R, D], F32,
                                 addr_space="Shared")
    dr["ycon"] = nc.dram_tensor("ycon", [NTOK, 128], F32)
    dr["yg"] = nc.dram_tensor("yg", [NCORE * NTOK, 128], F32,
                              addr_space="Shared")
    if debug:
        dr["dbgt"] = nc.dram_tensor("dbgt", [6, 128, H], F32,
                                    kind="ExternalOutput")
        dr["agcp"] = nc.dram_tensor("agcp", [PKG_R, D], F32,
                                    kind="ExternalOutput")
        dr["yccp"] = nc.dram_tensor("yccp", [NTOK, 128], F32,
                                    kind="ExternalOutput")
    dr["out"] = nc.dram_tensor("out", [TPC, D], F32, kind="ExternalOutput")

    agin_h = dr["agin"].ap().tensor
    agout_h = dr["agout"].ap().tensor
    yg_h = dr["yg"].ap().tensor

    p = Plan()
    DMASEMS = ("qInit", "qWsl", "qX0", "qX1", "qX1S", "qSX0", "qSX1",
               "qXP", "qXF", "qGT", "qSC0", "qSC1", "qYS", "qGP", "qDB", "qDC")
    for s in ("sT", "sV", "sA", "cc") + DMASEMS:
        p.sem(s)

    stack = ExitStack()
    T = {}

    def sba(name, shape, dt=F32):
        T[name] = stack.enter_context(nc.sbuf_tensor("sb_" + name, shape, dt))

    def psa(name, shape, dt=F32):
        T[name] = stack.enter_context(nc.psum_tensor(name, shape, dt))

    sba("wqfc", [128, DCH, H], FP8)
    sba("wqp", [128, HCH, D], FP8)
    sba("qacc", [128, 32])
    sba("onescol", [128, 1])
    sba("onesrow", [1, 128])
    sba("gam", [128, 2])
    sba("gam2", [128, 2])
    sba("consts", [128, 4])
    sba("idf", [128, 128])
    sba("idb", [128, 128], BF16)
    sba("maskl", [128, 128])
    sba("masku", [128, 128])
    sba("wbc", [128, H])
    sba("bcol", [128, HCH])
    sba("wcol", [128, HCH])
    sba("wc4", [128, HCH * 4], BF16)
    sba("crow4", [4, D])
    sba("r1p", [128, 4])
    sba("hws", [128, 1024])
    sba("gpart", [1, 2])
    sba("r1t", [4, 128])
    sba("eta", [1, 1])
    sba("scc", [128, 2])
    sba("stats", [128, 32])
    sba("statx", [128, 16])
    sba("wvT", [128, DCH * 128])
    sba("bvbc", [128, 128])
    sba("yofft", [NT, 1], U32)
    sba("x0a", [128, D])
    sba("x0b", [128, D])
    sba("u", [128, D])
    sba("upair", [128, 2 * D], BF16)
    sba("uT", [128, 2 * DCH * 128], BF16)
    sba("hg", [128, H])
    sba("hpairh", [128, H], BF16)
    sba("hTh", [128, HCH * 128], BF16)
    sba("gout", [128, D])
    sba("xout", [128, D])
    sba("sq8", [128, 12])
    sba("xln", [128, D])
    sba("xct", [128, 2 * DCH * 128])
    sba("lad", [128, 12 * 128])
    sba("xlt", [128, DCH * 128])
    sba("gt01", [128, 128])

    for i in range(8):
        psa(f"ps{i}", [128, 512], F32)

    for sname in ("sT", "sV", "sA", "cc") + DMASEMS:
        p.sems[sname][0] = stack.enter_context(nc.semaphore(sname))

    PS = [T[f"ps{i}"] for i in range(8)]

    wsl = T["hg"][:]

    sgn1 = T["hpairh"][:]
    sgn2 = T["hTh"][:]

    usc_v = T["xout"][:]
    wq5_v = T["upair"][:].bitcast(FP8)

    def xfm_s(bf, dc, t0, n):
        o = bf * 2048 + dc * 256 + t0
        return T["hg"][:, o:o + n]

    def xfm_dst(bf):
        return T["hg"][:, bf * 2048:(bf + 1) * 2048].rearrange(
            "p (c t) -> p c t", c=DCH)

    def xtm_s(bf, tt, c0, n):
        o = bf * 2048 + tt * 1024 + c0
        return T["wbc"][:, o:o + n]

    def xtm_dst(bf):
        return T["wbc"][:, bf * 2048:(bf + 1) * 2048].rearrange(
            "p (a q) -> p a q", a=2)

    def gts_s(bf, g):
        o = bf * 384 + g * 128
        return T["u"][:, o:o + 128]

    def gts_dst(bf):
        return T["u"][:, bf * 384:bf * 384 + 384].rearrange(
            "p (g q) -> p g q", g=3)

    
    x1s_v = T["xln"][:]
    xlt_v = T["xlt"][:]
    ygt_v = T["xct"][:, 0:1024]
    gt01_v = T["gt01"][:]
    mm_v = T["gout"][:]
    w0T_v = T["upair"][:].bitcast(F32)[:, 0:1024]
    ys_v = T["hpairh"][:, 0:512].bitcast(F32)
    vs_v = T["hpairh"][:, 512:1536].bitcast(F32)
    pe_v = T["hTh"][:, 0:1024].bitcast(F32)
    p0s_v = pe_v[:, 0:128]
    p1s_v = pe_v[:, 128:256]
    e0s_v = pe_v[:, 256:384]
    e1s_v = pe_v[:, 384:512]

    marks = {}

    def M(key, val=None):
        if val is None:
            return marks.get(key, 0)
        marks[key] = val
        return val

    for nm in ("consts", "idf", "idb", "maskl", "masku", "wbc", "bcol",
               "wcol", "eta", "bvbc"):
        p.dma("sync", T[nm][:], dr[nm].ap(), "qInit")
    p.dma("sync", T["yofft"][:], dr["yoff"].ap(), "qInit")
    p.dma("sync", T["wvT"][:].rearrange("p (c m) -> p c m", c=DCH),
          dr["wvT"].ap().rearrange("(c p) m -> p c m", p=128), "qInit")
    ld0 = p.cnt("qInit")

    p.op("vector", lambda e: e.memset(T["onescol"][:], 1.0), inc="sV")
    p.op("vector", lambda e: e.memset(T["onesrow"][:], 1.0), inc="sV")
    p.wait("vector", "qInit", ld0)
    p.op("vector", lambda e: e.tensor_scalar(
        T["scc"][0:1, 0:1], T["eta"][:], -2.0 / (B * D), None,
        op0=ALU.mult), inc="sV")
    p.wait("scalar", "qInit", ld0)
    p.op("scalar", lambda e: e.activation(
        T["scc"][0:1, 1:2], T["eta"][:], AF.Sqrt, scale=2.0 / (B * D)),
        inc="sA")
    vl, al = p.cnt("sV"), p.cnt("sA")
    p.wait("tensor", "sV", vl)
    p.wait("tensor", "sA", al)
    p.op("tensor", lambda e: e.matmul(
        PS[7][:, 0:2], T["onesrow"][:], T["scc"][0:1, 0:2],
        start=True, stop=True), inc="sT")
    tl = p.cnt("sT")
    p.wait("vector", "sT", tl)
    p.op("vector", lambda e: e.tensor_copy(T["scc"][:, 0:2], PS[7][:, 0:2]),
         inc="sV")

    p.wait("scalar", "sV", p.cnt("sV"))
    p.wait("sync", "sA", p.cnt("sA"))
    a_abs = {}
    SL = [("wfcT", 0, H)] + [("wpT", s, D) for s in range(4)]
    def slab_src(which, s):
        if which == "wfcT":
            return dr["wslab"].ap()[0:128, :]
        return dr["wslab"].ap()[128:256, :].rearrange(
            "p (q c) -> p q c", q=4)[:, s, :]
    for si, (which, s, cols) in enumerate(SL):
        p.wait("sync", "sA", a_abs.get(si - 1, 0))
        ld = p.dma("sync", wsl[:, 0:cols], slab_src(which, s), "qWsl")
        p.wait("scalar", "qWsl", ld)
        a_abs[si] = p.op("scalar", lambda e, si=si, n=cols: e.activation(
            sgn1[:, 0:n], wsl[:, 0:n], AF.Abs,
            accum_out=T["qacc"][:, si:si + 1]), inc="sA")
    p.wait("vector", "sA", p.cnt("sA"))
    p.op("vector", lambda e: e.tensor_reduce(
        T["stats"][:, 16:17], T["qacc"][:, 0:1], axis=AX.X, op=ALU.add),
        inc="sV")
    p.op("vector", lambda e: e.tensor_reduce(
        T["stats"][:, 17:18], T["qacc"][:, 1:5], axis=AX.X, op=ALU.add),
        inc="sV")
    p.wait("tensor", "sV", p.cnt("sV"))
    p.op("tensor", lambda e: e.matmul(
        PS[7][0:1, 4:5], T["stats"][:, 16:17], T["onescol"][:],
        start=True, stop=True), inc="sT")
    p.op("tensor", lambda e: e.matmul(
        PS[7][0:1, 5:6], T["stats"][:, 17:18], T["onescol"][:],
        start=False, stop=True, skip_group_check=True), inc="sT")
    p.wait("vector", "sT", p.cnt("sT"))
    p.op("vector", lambda e: e.tensor_copy(
        T["gpart"][0:1, 0:2], PS[7][0:1, 4:6]), inc="sV")
    vgp = p.cnt("sV")
    p.wait("gpsimd", "sV", vgp)
    p.dma("gpsimd", dr["gin"].ap(), T["gpart"][0:1, 0:2], "qDC")
    p.wait("gpsimd", "qDC", p.cnt("qDC"))
    p.op("gpsimd", lambda e: e.collective_compute(
        "AllReduce", ALU.add, replica_groups=[list(range(NCORE))],
        ins=[dr["gin"].ap().opt()], outs=[dr["gout"].ap().opt()]), inc="cc")
    p.wait("gpsimd", "cc", 1)
    ld = p.dma("gpsimd", T["gpart"][0:1, 0:2], dr["gout"].ap(), "qDC")
    p.wait("tensor", "qDC", ld)
    p.op("tensor", lambda e: e.matmul(
        PS[7][:, 4:6], T["onesrow"][:], T["gpart"][0:1, 0:2],
        start=True, stop=True), inc="sT")
    p.wait("vector", "sT", p.cnt("sT"))
    nel = float(D * H)
    p.op("vector", lambda e: e.tensor_scalar(
        T["gam"][:, 0:2], PS[7][:, 4:6], 0.5 / nel, 0.5 * EPS_ABS,
        op0=ALU.mult, op1=ALU.add), inc="sV")
    p.selfw("vector")
    p.op("vector", lambda e: e.tensor_scalar(
        T["gam2"][:, 0:2], T["gam"][:, 0:2], -1.0, None, op0=ALU.mult),
        inc="sV")
    p.wait("scalar", "sV", p.cnt("sV"))
    p.wait("sync", "sA", p.cnt("sA"))
    a_s2, v_q, q_st = {}, {}, {}
    for si, (which, s, cols) in enumerate(SL):
        p.wait("sync", "sA", a_s2.get(si - 1, 0))
        ld = p.dma("sync", wsl[:, 0:cols], slab_src(which, s), "qWsl")
        p.wait("scalar", "qWsl", ld)
        gcol = 0 if which == "wfcT" else 1
        p.wait("scalar", "sV", v_q.get(si - 2, 0))
        p.op("scalar", lambda e, n=cols, g=gcol: e.activation(
            sgn1[:, 0:n], wsl[:, 0:n], AF.Sign,
            bias=T["gam"][:, g:g + 1], scale=1.0), inc="sA")
        a_s2[si] = p.op("scalar", lambda e, n=cols, g=gcol: e.activation(
            sgn2[:, 0:n], wsl[:, 0:n], AF.Sign,
            bias=T["gam2"][:, g:g + 1], scale=1.0), inc="sA")
        p.wait("vector", "sA", a_s2[si])
        p.wait("vector", "qDB", q_st.get(si - 1, 0))
        v_q[si] = p.op("vector", lambda e, n=cols, si=si:
                       e.tensor_tensor(wq5_v[:, 0:n], sgn1[:, 0:n],
                                       sgn2[:, 0:n], op=ALU.add), inc="sV")
        p.wait("sync", "sV", v_q[si])
        if which == "wfcT":
            dst = dr["wqin"].ap()[0:128, :]
        else:
            dst = dr["wqin"].ap()[128:256, :].rearrange(
                "p (s c) -> p s c", s=4)[:, s, :]
        q_st[si] = p.dma("sync", dst, wq5_v[:, 0:cols], "qDB")
    p.wait("gpsimd", "qDB", p.cnt("qDB"))
    p.op("gpsimd", lambda e: e.collective_compute(
        "AllGather", ALU.bypass, replica_groups=[list(range(NCORE))],
        ins=[dr["wqin"].ap().opt()], outs=[dr["wqout"].ap().opt()]), inc="cc")
    p.wait("sync", "cc", 2)
    for dc in range(DCH):
        p.dma("sync", T["wqfc"][:, dc, :],
              dr["wqout"].ap()[dc * 256:dc * 256 + 128, :], "qWsl")
    for j in range(NCORE):
        p.dma("sync", T["wqp"][:, 4 * j:4 * (j + 1), :],
              dr["wqout"].ap()[j * 256 + 128:j * 256 + 256, :].rearrange(
                  "p (s c) -> p s c", s=4), "qWsl")
    wql = p.cnt("qWsl")
    p.wait("tensor", "qWsl", wql)
    p.wait("vector", "qWsl", wql)
    v_quant = p.cnt("sV")

    p.wait("vector", "qInit", ld0)
    p.selfw("vector")
    p.op("vector", lambda e: e.tensor_copy(
        T["wc4"][:, 0:4 * HCH:4], T["wcol"][:]), inc="sV")
    p.op("vector", lambda e: e.tensor_copy(
        T["wc4"][:, 2:4 * HCH:4], T["bcol"][:]), inc="sV")
    p.selfw("vector")
    p.op("vector", lambda e: e.tensor_tensor(
        T["wc4"][:, 1:4 * HCH:4], T["wcol"][:], T["wc4"][:, 0:4 * HCH:4],
        op=ALU.subtract), inc="sV")
    p.op("vector", lambda e: e.tensor_tensor(
        T["wc4"][:, 3:4 * HCH:4], T["bcol"][:], T["wc4"][:, 2:4 * HCH:4],
        op=ALU.subtract), inc="sV")
    vb = p.cnt("sV")
    p.wait("tensor", "sV", max(vb, v_quant))
    for q in range(2):
        for hc in range(HCH):
            p.op("tensor", lambda e, hc=hc, q=q: e.matmul(
                PS[5 + q][0:4, 0:512], T["wc4"][:, hc * 4:hc * 4 + 4],
                T["wqp"][:, hc, q * 512:(q + 1) * 512],
                start=(hc == 0), stop=(hc == HCH - 1)), inc="sT")
    tn = p.cnt("sT")
    p.wait("vector", "sT", tn)
    for q in range(2):
        p.op("vector", lambda e, q=q: e.tensor_copy(
            T["crow4"][0:4, q * 512:(q + 1) * 512], PS[5 + q][0:4, 0:512]),
            inc="sV")
    v_c0 = p.cnt("sV")

    def emit_ln_stats(st, sb, nfeat, half_rsd):
        p.selfw("vector")
        p.op("vector", lambda e: e.tensor_scalar(
            st[:, sb + 1:sb + 2], st[:, sb + 0:sb + 1], -1.0 / nfeat, None,
            op0=ALU.mult), inc="sV")
        p.selfw("vector")
        p.op("vector", lambda e: e.tensor_tensor(
            st[:, sb + 2:sb + 3], st[:, sb + 1:sb + 2], st[:, sb + 1:sb + 2],
            op=ALU.mult), inc="sV")
        p.op("vector", lambda e: e.tensor_scalar(
            st[:, sb + 3:sb + 4], st[:, sb + 7:sb + 8], 1.0 / nfeat, None,
            op0=ALU.mult), inc="sV")
        p.wait("vector", "sV", p.cnt("sV"))
        p.op("vector", lambda e: e.tensor_tensor(
            st[:, sb + 3:sb + 4], st[:, sb + 3:sb + 4], st[:, sb + 2:sb + 3],
            op=ALU.subtract), inc="sV")
        p.wait("scalar", "sV", p.cnt("sV"))
        p.op("scalar", lambda e: e.activation(
            st[:, sb + 4:sb + 5], st[:, sb + 3:sb + 4], AF.Sqrt,
            bias=T["consts"][:, 0:1], scale=1.0), inc="sA")
        p.wait("vector", "sA", p.cnt("sA"))
        p.op("vector", lambda e: e.reciprocal(
            st[:, sb + 5:sb + 6], st[:, sb + 4:sb + 5]), inc="sV")
        p.wait("vector", "sV", p.cnt("sV"))
        if half_rsd:
            p.op("vector", lambda e: e.tensor_scalar(
                st[:, sb + 6:sb + 7], st[:, sb + 5:sb + 6], 0.5, None,
                op0=ALU.mult), inc="sV")
        else:
            p.op("vector", lambda e: e.tensor_tensor(
                st[:, sb + 6:sb + 7], st[:, sb + 1:sb + 2],
                st[:, sb + 5:sb + 6], op=ALU.mult), inc="sV")
        return p.cnt("sV")

    def dbg_dump(slot, src_ap, ncols, eng="sync", waitsem=None, waitval=None):
        if not debug:
            return
        if waitsem is not None:
            p.wait(eng, waitsem, waitval)
        p.dma(eng, dr["dbgt"].ap()[slot, :, 0:ncols], src_ap, "qDB")
        p.wait(eng, "qDB", p.cnt("qDB"))
        p.wait("vector", "qDB", p.cnt("qDB"))
        p.wait("scalar", "qDB", p.cnt("qDB"))

    def ffn_head2_loads(i):
        par = i % 2
        x1sl = x1s_v if par == 0 else xlt_v
        ygtl = T["xct"][:, par * 1024:(par + 1) * 1024]
        qSX = "qSX0" if par == 0 else "qSX1"
        p.wait("sync", "sV", M(f"x2a,{i-2}"))
        p.wait("sync", "qX1S", M(f"ldx1,{i-1}"))
        p.wait("sync", qSX, M(f"stx1,{i}"))
        ldx1 = p.dma("sync", x1sl,
                     dr["xmid"].ap()[i * 128:(i + 1) * 128, :], "qX1S")
        M(f"ldx1,{i}", ldx1)
        p.wait("gpsimd", "cc", 3 + NYS)
        p.wait("gpsimd", "sV", M(f"x2a,{i-2}"))
        p.wait("gpsimd", "qGP", M(f"ldy,{i-1}"))

        def g_ygt(e, regs):
            e.reg_load(regs["ry"], T["yofft"][i:i + 1, 0:1])
            return e.dma_start(
                out=ygtl.rearrange("p (a b) -> p a b", a=NCORE),
                in_=bass.AP(yg_h, regs["ry"],
                            [[128, 128], [YR * 128, NCORE], [1, 128]]))
        ldy = p.opr("gpsimd", g_ygt, inc="qGP", incby=16)
        M(f"ldy,{i}", ldy)

    def ffn_head(i, phase):
        par = i % 2
        sb = 8 * par
        sb2 = 16 + 8 * par
        x0 = T["x0a"] if par == 0 else T["x0b"]
        st = T["stats"]

        qX = "qX0" if par == 0 else "qX1"
        qSX = "qSX0" if par == 0 else "qSX1"
        if phase == 1:
            p.wait("sync", "sV", M(f"res{phase},{i-2}"))
            ldx = p.dma("sync", x0[:], dr["xr"].ap()[i * 128:(i + 1) * 128, :],
                        qX)
            p.wait("vector", qX, ldx)
            p.wait("scalar", qX, ldx)
        else:
            x1sl = x1s_v if par == 0 else xlt_v
            ygtl = T["xct"][:, par * 1024:(par + 1) * 1024]
            p.wait("vector", "qX1S", M(f"ldx1,{i}"))
            p.wait("vector", "qGP", M(f"ldy,{i}"))
            p.wait("vector", "sV", M(f"res{phase},{i-2}"))
            vx2 = p.op("vector", lambda e: e.tensor_tensor(
                x0[:], x1sl,
                ygtl, op=ALU.add),
                inc="sV")
            M(f"x2a,{i}", vx2)
            p.wait("scalar", "sV", vx2)

        p.selfw("vector")
        p.op("vector", lambda e: e.tensor_reduce(
            st[:, sb + 0:sb + 1], x0[:], axis=AX.X, op=ALU.add), inc="sV")
        p.wait("scalar", "sV", M(f"upair,{i-1}"))
        a_sq1 = p.op("scalar", lambda e: e.activation(
            T["u"][:], x0[:], AF.Square, accum_out=st[:, sb + 7:sb + 8]),
            inc="sA")
        p.wait("vector", "sA", a_sq1)
        v_st1 = emit_ln_stats(st, sb, D, half_rsd=True)
        p.op("vector", lambda e: e.tensor_scalar(
            T["u"][:], x0[:], st[:, sb + 1:sb + 2], None, op0=ALU.add),
            inc="sV")
        p.selfw("vector")
        p.wait("vector", "sT", M(f"uTt,{i-1}"))
        p.op("vector", lambda e: e.tensor_copy(
            T["upair"][:, 0:D], T["u"][:]), inc="sV")
        p.wait("vector", "sV", p.cnt("sV"))
        vup = p.op("vector", lambda e: e.tensor_tensor(
            T["upair"][:, D:2 * D], T["u"][:], T["upair"][:, 0:D],
            op=ALU.subtract), inc="sV")
        M(f"upair,{i}", vup)

        if debug and phase == 1 and i == 0:
            dbg_dump(0, x0[:], D, waitsem="sV", waitval=p.cnt("sV"))
            dbg_dump(1, T["u"][:], D)
            dbg_dump(2, T["stats"][:, 0:8], 8)
        M(f"vst1,{i}", v_st1)

    def ffn_body_ut(i, phase):
        par = i % 2
        sb = 8 * par
        sb2 = 16 + 8 * par
        x0 = T["x0a"] if par == 0 else T["x0b"]
        st = T["stats"]
        qSX = "qSX0" if par == 0 else "qSX1"
        v_st1 = M(f"vst1,{i}")
        vup = M(f"upair,{i}")
        p.wait("tensor", "sV", vup)
        p.wait("tensor", "sA", M("b45free"))
        p.wait("tensor", "sV", max(M(f"hTev,{i-1},6"), M(f"hTev,{i-1},7")))
        tt = None
        for b in range(16):
            bank, slot = 4 + b // 8, b % 8
            tt = p.op("tensor", lambda e, b=b, bank=bank, slot=slot:
                      e.transpose(
                          PS[bank][:].bitcast(BF16)[:,
                                                    slot * 128:(slot + 1) * 128],
                          T["upair"][:, b * 128:(b + 1) * 128], T["idb"][:]),
                      inc="sT")
        M(f"uTt,{i}", tt)
        p.wait("scalar", "sT", tt)
        p.selfw("scalar")
        ve = None
        for b45 in range(2):
            ve = p.op("scalar", lambda e, b45=b45: e.activation(
                T["uT"][:, b45 * 1024:(b45 + 1) * 1024],
                PS[4 + b45][:].bitcast(BF16), AF.Copy, scale=1.0), inc="sA")
        M("b45free", ve)
        M(f"uTe,{i}", ve)

    def fc_group(i, pc, phase):
        bank = pc % 2
        if pc >= 2:
            p.wait("tensor", "sA", M(f"gelu,{i},{pc-2}"))
        else:
            p.wait("tensor", "sA", M(f"gelu,{i-1},{pc+6}"))
        for dc in range(DCH):
            for hl in range(2):
                p.op("tensor",
                     lambda e, pc=pc, dc=dc, hl=hl, bank=bank:
                     e.matmul(
                         PS[bank][:],
                         T["uT"][:, (hl * DCH + dc) * 128:
                                 (hl * DCH + dc + 1) * 128],
                         T["wqfc"][:, dc, pc * 512:(pc + 1) * 512],
                         start=(dc == 0 and hl == 0),
                         stop=(dc == DCH - 1 and hl == 1)),
                     inc="sT")
        return M(f"tfc{phase},{i},{pc}", p.cnt("sT"))

    def ffn_body(i, phase, nxt=None):
        par = i % 2
        sb = 8 * par
        sb2 = 16 + 8 * par
        x0 = T["x0a"] if par == 0 else T["x0b"]
        st = T["stats"]
        qSX = "qSX0" if par == 0 else "qSX1"
        v_st1 = M(f"vst1,{i}")
        ve = M(f"uTe,{i}")
        if i == 0:
            p.selfw("scalar")
        if phase == 1 and i == 0:
            p.wait("tensor", "sV", v_c0)
        p.wait("tensor", "sA", ve)
        p.wait("scalar", "sV", v_st1)
        for pc in range(10):
            if pc < 8:
                if f"tfc{phase},{i},{pc}" in marks:
                    tfc = M(f"tfc{phase},{i},{pc}")
                else:
                    tfc = fc_group(i, pc, phase)
                bank = pc % 2
                p.wait("scalar", "sT", tfc)
                if pc == 0:
                    p.selfw("scalar")
                    p.wait("scalar", "sV", M(f"sg2add,{i-1}"))
                hwgate = (M(f"hwm,{i},{pc-8}") if pc >= 8 else
                          M(f"hwm,{i-1},{pc}"))
                p.wait("scalar", "sV", hwgate)
                ag = p.op("scalar", lambda e, pc=pc, bank=bank: e.activation(
                    T["hg"][:, pc * 512:(pc + 1) * 512], PS[bank][:], AF.Gelu,
                    scale=st[:, sb + 6:sb + 7],
                    accum_out=T["sq8"][:, pc:pc + 1]), inc="sA")
                M(f"gelu,{i},{pc}", ag)
                sl = pc % 4
                uo = (pc % 2) * 512
                p.wait("vector", "sA", ag)
                p.selfw("vector")
                vw = p.op("vector", lambda e, pc=pc, uo=uo: e.tensor_tensor(
                    T["hws"][:, uo:uo + 512], T["hg"][:, pc * 512:(pc + 1) * 512],
                    T["wbc"][:, pc * 512:(pc + 1) * 512], op=ALU.mult),
                    inc="sV")
                M(f"hwm,{i},{pc}", vw)
                spgate = (M(f"hTt,{i},{pc-4}") if pc >= 4 else
                          M(f"hTt,{i-1},{pc+4}"))
                p.wait("vector", "sT", spgate)
                p.selfw("vector")
                p.op("vector", lambda e, sl=sl, uo=uo: e.tensor_copy(
                    T["hpairh"][:, sl * 1024:sl * 1024 + 512],
                    T["hws"][:, uo:uo + 512]), inc="sV")
                p.selfw("vector")
                vsp = p.op("vector", lambda e, sl=sl, uo=uo: e.tensor_tensor(
                    T["hpairh"][:, sl * 1024 + 512:sl * 1024 + 1024],
                    T["hws"][:, uo:uo + 512],
                    T["hpairh"][:, sl * 1024:sl * 1024 + 512],
                    op=ALU.subtract), inc="sV")
                M(f"hsplit,{i},{pc}", vsp)
                if pc % 2 == 1:
                    qt = pc // 2
                    if qt == 0:
                        p.wait("scalar", "sV", M(f"res{phase},{i-1}"))
                        p.wait("scalar", "sV", M(f"sg2r,{i-1}"))
                    p.selfw("scalar")
                    p.op("scalar", lambda e, qt=qt: e.activation(
                        T["gout"][:], T["hg"][:, qt * D:(qt + 1) * D],
                        AF.Square, accum_out=T["sq8"][:, 8 + qt:9 + qt]),
                        inc="sA")
                    M(f"sqq,{i},{qt}", p.cnt("sA"))
                if pc == 2 and nxt is not None:
                    nxt()
            if 1 <= pc <= 8:
                pt = pc - 1
                sl = pt % 4
                b45 = 4 + pt % 2
                p.wait("tensor", "sV", M(f"hsplit,{i},{pt}"))
                if pt >= 2:
                    p.wait("tensor", "sV", M(f"hTev,{i},{pt-2}"))
                else:
                    p.wait("tensor", "sA", M(f"uTe,{i}"))
                    p.wait("tensor", "sV", M(f"hTev,{i-1},{pt+6}"))
                ttp = None
                for blk in range(8):
                    ttp = p.op("tensor",
                               lambda e, sl=sl, blk=blk, b45=b45:
                               e.transpose(
                                   PS[b45][:].bitcast(BF16)[
                                       :, blk * 128:(blk + 1) * 128],
                                   T["hpairh"][:, sl * 1024 + blk * 128:
                                               sl * 1024 + (blk + 1) * 128],
                                   T["idb"][:]), inc="sT")
                M(f"hTt,{i},{pt}", ttp)
                p.wait("vector", "sT", ttp)
                pg = (M(f"projg,{i},{pt-4}") if pt >= 4 else
                      M(f"projg,{i-1},{pt+4}"))
                p.wait("vector", "sT", pg)
                vep = p.op("vector", lambda e, sl=sl, b45=b45: e.tensor_copy(
                    T["hTh"][:, sl * 1024:(sl + 1) * 1024],
                    PS[b45][:].bitcast(BF16)), inc="sV")
                M(f"hTev,{i},{pt}", vep)
            if pc >= 2:
                pp = pc - 2
                sl = pp % 4
                p.wait("tensor", "sV", M(f"hTev,{i},{pp}"))
                if pp == 0:
                    p.wait("tensor", "sV", M(f"fev,{i-1}"))
                for q in range(2):
                    for hl in range(2):
                        for hcl in range(4):
                            tci = sl * 8 + hl * 4 + hcl
                            hcg = pp * 4 + hcl
                            p.op("tensor",
                                 lambda e, q=q, tci=tci, hcg=hcg, pp=pp,
                                 hl=hl, hcl=hcl:
                                 e.matmul(
                                     PS[2 + q][:],
                                     T["hTh"][:, tci * 128:(tci + 1) * 128],
                                     T["wqp"][:, hcg,
                                              q * 512:(q + 1) * 512],
                                     start=(pp == 0 and hl == 0 and hcl == 0),
                                     stop=False), inc="sT")
                M(f"projg,{i},{pp}", p.cnt("sT"))
        p.wait("vector", "sA", M(f"gelu,{i},7"))
        p.selfw("vector")
        vg1 = p.op("vector", lambda e: e.tensor_reduce(
            st[:, sb2 + 0:sb2 + 1], T["sq8"][:, 0:8], axis=AX.X, op=ALU.add),
            inc="sV")
        p.wait("vector", "sA", M(f"sqq,{i},3"))
        p.op("vector", lambda e: e.tensor_reduce(
            st[:, sb2 + 7:sb2 + 8], T["sq8"][:, 8:12], axis=AX.X, op=ALU.add),
            inc="sV")
        M(f"sg2r,{i}", p.cnt("sV"))
        M(f"sg2add,{i}", p.cnt("sV"))
        if debug and phase == 1 and i == 0:
            dbg_dump(3, T["hg"][:], H, waitsem="sA", waitval=p.cnt("sA"))
        v_st2 = emit_ln_stats(st, sb2, H, half_rsd=False)
        p.op("vector", lambda e: e.tensor_scalar(
            st[:, sb2 + 3:sb2 + 4], st[:, sb2 + 5:sb2 + 6], 0.25, None,
            op0=ALU.mult), inc="sV")
        p.selfw("vector")
        p.wait("vector", "sT", M("r1tt"))
        for c_ in range(2):
            p.op("vector", lambda e, c_=c_: e.tensor_copy(
                T["r1p"][:, c_:c_ + 1], st[:, sb2 + 1:sb2 + 2]), inc="sV")
            p.op("vector", lambda e, c_=c_: e.tensor_copy(
                T["r1p"][:, 2 + c_:3 + c_], st[:, sb2 + 4:sb2 + 5]), inc="sV")
        vr1 = p.cnt("sV")
        if i + 1 < NT:
            ffn_body_ut(i + 1, phase)
            p.wait("tensor", "sA", M(f"uTe,{i+1}"))
            fc_group(i + 1, 0, phase)
        p.wait("tensor", "sV", vr1)
        p.wait("tensor", "sV", M("b6free"))
        tr1 = p.op("tensor", lambda e: e.transpose(
            PS[6][0:4, 0:128], T["r1p"][:], T["idf"][:]), inc="sT")
        M("r1tt", tr1)
        if i + 1 < NT:
            fc_group(i + 1, 1, phase)
        p.wait("vector", "sT", tr1)
        vr1e = p.op("vector", lambda e: e.tensor_copy(
            T["r1t"][0:4, 0:128], PS[6][0:4, 0:128]), inc="sV")
        M("b6free", vr1e)
        p.wait("tensor", "sV", vr1e)
        tc0 = None
        for q in range(2):
            tc0 = p.op("tensor", lambda e, q=q: e.matmul(
                PS[2 + q][:], T["r1t"][0:4, 0:128],
                T["crow4"][0:4, q * 512:(q + 1) * 512], start=False,
                stop=True), inc="sT")
        p.wait("vector", "sT", tc0)
        p.selfw("vector")
        af = None
        for q in range(2):
            af = p.op("vector", lambda e, q=q: e.tensor_scalar(
                T["gout"][:, q * 512:(q + 1) * 512], PS[2 + q][:],
                st[:, sb2 + 3:sb2 + 4], None, op0=ALU.mult), inc="sV")
        M(f"fev,{i}", af)
        if debug and phase == 1 and i == 0:
            dbg_dump(5, T["gout"][:], D, waitsem="sV", waitval=p.cnt("sV"))
        qSXo = "qSX1" if par == 0 else "qSX0"
        p.wait("vector", qSXo, M(f"stx{phase},{i-1}"))
        p.selfw("vector")
        vres = p.op("vector", lambda e: e.tensor_tensor(
            T["xout"][:], x0[:], T["gout"][:], op=ALU.add), inc="sV")
        M(f"res{phase},{i}", vres)
        p.wait("sync", "sV", vres)
        dst = dr["xmid"] if phase == 1 else dr["out"]
        stx = p.dma("sync", dst.ap()[i * 128:(i + 1) * 128, :], T["xout"][:],
                    qSX)
        M(f"stx{phase},{i}", stx)
        return vres

    def x_epilogue(i):
        par = i % 2
        sbx = 8 * par
        st = T["statx"]
        vres = M(f"res1,{i}")
        p.wait("scalar", "sV", M(f"xlnw,{i-2}"))
        p.wait("scalar", "sV", vres)
        p.selfw("scalar")
        p.op("scalar", lambda e: e.activation(
            T["xln"][:], T["xout"][:], AF.Copy, scale=1.0,
            accum_out=st[:, sbx + 0:sbx + 1]), inc="sA")
        p.selfw("scalar")
        a_sq = p.op("scalar", lambda e: e.activation(
            T["xln"][:], T["xout"][:], AF.Square,
            accum_out=st[:, sbx + 7:sbx + 8]), inc="sA")
        p.wait("vector", "sA", a_sq)
        p.selfw("vector")
        p.op("vector", lambda e: e.tensor_scalar(
            st[:, sbx + 1:sbx + 2], st[:, sbx + 0:sbx + 1], -1.0 / D, None,
            op0=ALU.mult), inc="sV")
        p.selfw("vector")
        p.op("vector", lambda e: e.tensor_tensor(
            st[:, sbx + 2:sbx + 3], st[:, sbx + 1:sbx + 2],
            st[:, sbx + 1:sbx + 2], op=ALU.mult), inc="sV")
        p.op("vector", lambda e: e.tensor_scalar(
            st[:, sbx + 3:sbx + 4], st[:, sbx + 7:sbx + 8], 1.0 / D, None,
            op0=ALU.mult), inc="sV")
        p.wait("vector", "sV", p.cnt("sV"))
        p.op("vector", lambda e: e.tensor_tensor(
            st[:, sbx + 3:sbx + 4], st[:, sbx + 3:sbx + 4],
            st[:, sbx + 2:sbx + 3], op=ALU.subtract), inc="sV")
        p.wait("scalar", "sV", p.cnt("sV"))
        p.op("scalar", lambda e: e.activation(
            st[:, sbx + 4:sbx + 5], st[:, sbx + 3:sbx + 4], AF.Sqrt,
            bias=T["consts"][:, 0:1], scale=1.0), inc="sA")
        p.wait("vector", "sA", p.cnt("sA"))
        p.op("vector", lambda e: e.reciprocal(
            st[:, sbx + 5:sbx + 6], st[:, sbx + 4:sbx + 5]), inc="sV")
        p.wait("vector", "sV", p.cnt("sV"))
        p.op("vector", lambda e: e.tensor_tensor(
            st[:, sbx + 6:sbx + 7], st[:, sbx + 1:sbx + 2],
            st[:, sbx + 5:sbx + 6], op=ALU.mult), inc="sV")
        p.wait("vector", "sV", p.cnt("sV"))
        vx = p.op("vector", lambda e: e.tensor_scalar(
            T["xln"][:], T["xout"][:], st[:, sbx + 5:sbx + 6],
            st[:, sbx + 6:sbx + 7], op0=ALU.mult, op1=ALU.add), inc="sV")
        M(f"xlnw,{i}", vx)
        M(f"xtms,{i}", 0)
        p.wait("tensor", "sV", vx)
        p.wait("vector", "qXF", M(f"xlts,{i-1}"))
        p.wait("tensor", "sV", M("b6free"))
        for b in range(4):
            p.op("tensor", lambda e, b=b: e.transpose(
                PS[6][:, b * 128:(b + 1) * 128],
                T["xln"][:, b * 128:(b + 1) * 128], T["idf"][:]),
                inc="sT")
        p.wait("tensor", "sV", M("b7free"))
        tt2 = None
        for b in range(4):
            tt2 = p.op("tensor", lambda e, b=b: e.transpose(
                PS[7][:, b * 128:(b + 1) * 128],
                T["xln"][:, (4 + b) * 128:(5 + b) * 128], T["idf"][:]),
                inc="sT")
        p.wait("vector", "sT", tt2)
        p.selfw("vector")
        ve6 = p.op("vector", lambda e: e.tensor_copy(
            xlt_v[:, 0:512], PS[6][:]), inc="sV")
        ve = p.op("vector", lambda e: e.tensor_copy(
            xlt_v[:, 512:1024], PS[7][:]), inc="sV")
        M("b6free", ve6)
        M("b7free", ve)
        p.wait("sync", "sV", ve)
        stf = p.dma("sync", bass.AP(agin_h, XFM_R0 * D + i * 128,
                                    [[TPC, 128], [128 * TPC, DCH], [1, 128]]),
                    xlt_v.rearrange("p (c m) -> p c m", c=DCH), "qXF")
        M(f"xlts,{i}", stf)
        p.wait("scalar", "sV", M(f"gdone,{(i//2)-1}"))
        p.wait("scalar", "sV", ve)
        vxc = p.op("scalar", lambda e, par=par: e.activation(
            T["xct"][:, par * 1024:(par + 1) * 1024], xlt_v, AF.Copy,
            scale=T["scc"][:, 1:2]), inc="sA")
        M(f"xct,{i}", vxc)

    LSQ = {2: (1, 0), 3: (0, 1), 4: (3, 2), 5: (2, 3), 6: (5, 4),
           7: (4, 5), 8: (7, 6), 9: (6, 7), 10: (9, 8)}

    def gt_package(l):
        p.wait("tensor", "sA", M(f"xct,{2*l}"))
        p.wait("tensor", "sA", M(f"xct,{2*l+1}"))

        def gmm(bank, pa, pb):
            tg = None
            for dc in range(DCH):
                tg = p.op("tensor", lambda e, dc=dc, bank=bank, pa=pa, pb=pb:
                          e.matmul(PS[bank][:, 0:128],
                                   T["xct"][:, pa * 1024 + dc * 128:
                                             pa * 1024 + (dc + 1) * 128],
                                   T["xct"][:, pb * 1024 + dc * 128:
                                             pb * 1024 + (dc + 1) * 128],
                                   start=(dc == 0), stop=(dc == DCH - 1)),
                          inc="sT")
            return tg

        p.wait("tensor", "sV", M("b7free"))
        tg00 = gmm(7, 0, 0)
        p.wait("vector", "sT", tg00)
        p.wait("vector", "qGT", p.cnt("qGT"))
        p.op("vector", lambda e: e.tensor_tensor(
            T["lad"][:, 0:128], PS[7][:, 0:128], T["maskl"][:], op=ALU.mult),
            inc="sV")
        vN0 = p.op("vector", lambda e: e.tensor_tensor(
            T["lad"][:, 128:256], PS[7][:, 0:128], T["masku"][:], op=ALU.mult),
            inc="sV")
        M("b7free", vN0)

        def ladder(blk, bank):
            lad = T["lad"]
            p.selfw("vector")
            p.op("vector", lambda e: e.tensor_tensor(
                lad[:, 11 * 128:12 * 128], lad[:, 128:256], T["idf"][:],
                op=ALU.add), inc="sV")
            p.wait("tensor", "sV", p.cnt("sV"))
            for dstslot in range(2, 11):
                a, b = LSQ[dstslot]
                tq = p.op("tensor", lambda e, a=a, b=b, bank=bank: e.matmul(
                    PS[bank][:, 0:128], T["lad"][:, a * 128:(a + 1) * 128],
                    T["lad"][:, b * 128:(b + 1) * 128],
                    start=True, stop=True), inc="sT")
                p.wait("vector", "sT", tq)
                vq2 = p.op("vector", lambda e, d=dstslot, bank=bank:
                           e.tensor_copy(T["lad"][:, d * 128:(d + 1) * 128],
                                         PS[bank][:, 0:128]), inc="sV")
                p.wait("tensor", "sV", vq2)
            for nslot in (2, 4, 6, 8, 10):
                p.selfw("vector")
                p.op("tensor", lambda e, bank=bank: e.matmul(
                    PS[bank][:, 0:128], T["idf"][:],
                    T["lad"][:, 11 * 128:12 * 128],
                    start=True, stop=False), inc="sT")
                tq = p.op("tensor", lambda e, n=nslot, bank=bank: e.matmul(
                    PS[bank][:, 0:128], T["lad"][:, n * 128:(n + 1) * 128],
                    T["lad"][:, 11 * 128:12 * 128],
                    start=False, stop=True), inc="sT")
                p.wait("vector", "sT", tq)
                vq2 = p.op("vector", lambda e, bank=bank: e.tensor_copy(
                    T["lad"][:, 11 * 128:12 * 128], PS[bank][:, 0:128]),
                    inc="sV")
                p.wait("tensor", "sV", vq2)
            p.wait("sync", "sV", p.cnt("sV"))
            return p.dma("sync",
                         bass.AP(agin_h, (GT_R0 + l * 48 + blk * 16) * D,
                                 [[128, 128], [1, 128]]),
                         T["lad"][:, 11 * 128:12 * 128], "qGT")

        st0 = ladder(0, 7)
        M("b7free", p.cnt("sV"))
        p.wait("tensor", "sV", M("b6free"))
        tg11 = gmm(6, 1, 1)
        p.wait("vector", "sT", tg11)
        p.wait("vector", "qGT", st0)
        p.op("vector", lambda e: e.tensor_tensor(
            T["lad"][:, 0:128], PS[6][:, 0:128], T["maskl"][:], op=ALU.mult),
            inc="sV")
        vN1 = p.op("vector", lambda e: e.tensor_tensor(
            T["lad"][:, 128:256], PS[6][:, 0:128], T["masku"][:], op=ALU.mult),
            inc="sV")
        M("b6free", vN1)
        ladder(1, 6)
        M("b6free", p.cnt("sV"))
        p.wait("tensor", "sV", M("b7free"))
        tg01 = gmm(7, 0, 1)
        p.wait("vector", "sT", tg01)
        p.wait("vector", "qGT", p.cnt("qGT"))
        vg01 = p.op("vector", lambda e: e.tensor_scalar(
            gt01_v, PS[7][:, 0:128], -1.0, None, op0=ALU.mult),
            inc="sV")
        M("b7free", vg01)
        p.wait("sync", "sV", vg01)
        stg = p.dma("sync", bass.AP(agin_h, (GT_R0 + l * 48 + 32) * D,
                                    [[128, 128], [1, 128]]), gt01_v, "qGT")
        M(f"gtst,{l}", stg)
        M(f"gdone,{l}", p.cnt("sV"))

    ffn_head(0, 1)
    ffn_body_ut(0, 1)
    for i in range(NT):
        nx = (lambda i=i: ffn_head(i + 1, 1)) if i + 1 < NT else None
        ffn_body(i, 1, nxt=nx)
        x_epilogue(i)
        if i % 2 == 1:
            gt_package(i // 2)

    for q in ("qXF", "qGT"):
        p.wait("gpsimd", q, p.cnt(q))
    p.op("gpsimd", lambda e: e.collective_compute(
        "AllGather", ALU.bypass, replica_groups=[list(range(NCORE))],
        ins=[dr["agin"].ap().opt()], outs=[dr["agout"].ap().opt()]),
        inc="cc")

    p.wait("sync", "cc", 3)
    p.wait("vector", "cc", 3)
    p.wait("vector", "qSX0", p.cnt("qSX0"))
    p.wait("vector", "qSX1", p.cnt("qSX1"))
    ldw0 = p.dma("sync", w0T_v.rearrange("p (c m) -> p c m", c=DCH),
                 dr["w0T"].ap().rearrange("(c p) m -> p c m", p=128), "qInit")
    p.wait("vector", "qInit", ldw0)
    p.op("vector", lambda e: e.tensor_tensor(
        mm_v, w0T_v, T["wvT"][:], op=ALU.subtract), inc="sV")
    v_minit = p.cnt("sV")

    def chunk_loads(t):
        j, l = t // CPC, t % CPC
        bf = t % 2
        p.wait("sync", "sT", M(f"scU,{t-2}"))
        qsc = "qSC0" if bf == 0 else "qSC1"
        base = j * PKG_R * D
        p.dma("sync", xfm_dst(bf),
              bass.AP(agout_h, base + XFM_R0 * D + l * CH,
                      [[TPC, 128], [128 * TPC, DCH], [1, CH]]), qsc)
        ldg = p.dma("sync", gts_dst(bf),
                    bass.AP(agout_h, base + (GT_R0 + l * 48) * D,
                            [[128, 128], [16 * D, 3], [1, 128]]), qsc)
        M(f"scld,{t}", ldg)

    chunk_loads(0)
    chunk_loads(1)

    VB = (PS[0][:, 0:128], PS[0][:, 128:256])
    PB = (PS[1][:, 0:128], PS[1][:, 128:256])
    EB = (PS[2][:, 0:128], PS[2][:, 128:256])
    UBK = (PS[3], PS[4])

    def xtm_half(u, tt):
        if u >= NCHUNK - 1:
            return
        bf = u % 2
        if tt == 0:
            p.wait("tensor", "qSC0" if bf == 0 else "qSC1", M(f"scld,{u}"))
        p.wait("tensor", "sA", M("xtmb56"))
        tq = None
        for dc in range(DCH):
            bank = 5 + dc // 4
            tq = p.op("tensor", lambda e, bf=bf, tt=tt, dc=dc, bank=bank:
                      e.transpose(
                          PS[bank][:, (dc % 4) * 128:(dc % 4 + 1) * 128],
                          xfm_s(bf, dc, tt * 128, 128), T["idf"][:]),
                      inc="sT")
        p.wait("scalar", "sT", tq)
        p.wait("scalar", "sT", M(f"scU,{u-2}"))
        ae = None
        for bk in range(2):
            ae = p.op("scalar", lambda e, bf=bf, tt=tt, bk=bk:
                      e.activation(
                          xtm_s(bf, tt, bk * 512, 512),
                          PS[5 + bk][:, 0:512], AF.Copy, scale=1.0),
                      inc="sA")
        M("xtmb56", ae)
        if tt == 1:
            M(f"xtme,{u}", p.cnt("sA"))

    def v_group(t, tt):
        if t >= NCHUNK:
            return
        bf = t % 2
        if tt == 0:
            p.wait("tensor", "qSC0" if bf == 0 else "qSC1", M(f"scld,{t}"))
            p.wait("tensor", "sV", M(f"scVe,{t-1}"))
        tv = None
        for dc in range(DCH):
            tv = p.op("tensor", lambda e, tt=tt, dc=dc, bf=bf: e.matmul(
                VB[tt], xfm_s(bf, dc, tt * 128, 128),
                T["wvT"][:, dc * 128:(dc + 1) * 128],
                start=(dc == 0), stop=(dc == DCH - 1)), inc="sT")
        M(f"scV,{t},{tt}", tv)

    def scan_chunk(t):
        bf = t % 2
        vsl = vs_v[:, bf * 256:(bf + 1) * 256]
        p.wait("tensor", "qSC0" if bf == 0 else "qSC1", M(f"scld,{t}"))
        p.wait("tensor", "sV", M(f"scM,{t-1}") if t > 0 else v_minit)
        p.wait("tensor", "sV", M(f"scPe,{t-1}"))
        for dc in range(DCH):
            tp_ = p.op("tensor", lambda e, dc=dc, bf=bf: e.matmul(
                PB[0], xfm_s(bf, dc, 0, 128),
                mm_v[:, dc * 128:(dc + 1) * 128], start=(dc == 0),
                stop=(dc == DCH - 1)), inc="sT")
        M(f"scP,{t},0", tp_)
        xtm_half(t + 1, 0)
        p.wait("vector", "sT", M(f"scV,{t},1"))
        p.wait("vector", "qYS", M(f"scYq,{t-1}"))
        vv = None
        for tt in range(2):
            vv = p.op("vector", lambda e, tt=tt, vsl=vsl: e.tensor_tensor(
                vsl[:, tt * 128:(tt + 1) * 128], VB[tt],
                T["bvbc"][:], op=ALU.add), inc="sV")
        M(f"scVe,{t}", vv)
        p.wait("vector", "sT", M(f"scP,{t},0"))
        vp0 = p.op("vector", lambda e: e.tensor_tensor(
            p0s_v, PB[0], T["bvbc"][:], op=ALU.subtract),
            inc="sV")
        p.wait("tensor", "sV", vp0)
        p.wait("tensor", "sV", M(f"scEe,{t-1}"))
        te0 = p.op("tensor", lambda e, bf=bf: e.matmul(
            EB[0], gts_s(bf, 0), p0s_v, start=True,
            stop=True), inc="sT")
        for dc in range(DCH):
            tp_ = p.op("tensor", lambda e, dc=dc, bf=bf: e.matmul(
                PB[1], xfm_s(bf, dc, 128, 128),
                mm_v[:, dc * 128:(dc + 1) * 128], start=(dc == 0),
                stop=(dc == DCH - 1)), inc="sT")
        M(f"scP,{t},1", tp_)
        p.wait("vector", "sT", te0)
        p.selfw("vector")
        ve0 = p.op("vector", lambda e: e.tensor_copy(
            e0s_v[:], EB[0]), inc="sV")
        p.op("vector", lambda e, vsl=vsl: e.tensor_tensor(
            ys_v[:, 0:128], EB[0], vsl[:, 0:128],
            op=ALU.add), inc="sV")
        p.wait("tensor", "sV", ve0)
        tc1 = p.op("tensor", lambda e, bf=bf: e.matmul(
            PB[1], gts_s(bf, 2), e0s_v[:], start=False,
            stop=True, skip_group_check=True), inc="sT")
        v_group(t + 1, 0)
        p.wait("vector", "sT", tc1)
        vp1 = p.op("vector", lambda e: e.tensor_tensor(
            p1s_v, PB[1], T["bvbc"][:], op=ALU.subtract),
            inc="sV")
        M(f"scPe,{t}", vp1)
        p.wait("tensor", "sV", vp1)
        te1 = p.op("tensor", lambda e, bf=bf: e.matmul(
            EB[1], gts_s(bf, 1), p1s_v, start=True,
            stop=True), inc="sT")
        xtm_half(t + 1, 1)
        p.wait("vector", "sT", te1)
        p.selfw("vector")
        ve1 = p.op("vector", lambda e: e.tensor_copy(
            e1s_v[:], EB[1]), inc="sV")
        vy1 = p.op("vector", lambda e, vsl=vsl: e.tensor_tensor(
            ys_v[:, 128:256], EB[1], vsl[:, 128:256],
            op=ALU.add), inc="sV")
        M(f"scEe,{t}", ve1)
        p.wait("sync", "sV", vy1)
        p.dma("sync", dr["ycon"].ap()[t * CH:t * CH + 128, :],
              ys_v[:, 0:128], "qYS")
        yq = p.dma("sync", dr["ycon"].ap()[t * CH + 128:(t + 1) * CH, :],
                   ys_v[:, 128:256], "qYS")
        M(f"scYq,{t}", yq)
        if t + 1 >= NCHUNK:
            return
        p.wait("tensor", "sV", ve1)
        p.wait("tensor", "sA", M(f"xtme,{t}"))
        p.wait("tensor", "sV", M(f"scUe,{t-1}"))
        tu = None
        for tt in range(2):
            es = e0s_v if tt == 0 else e1s_v
            for dc in range(DCH):
                tu = p.op("tensor", lambda e, tt=tt, dc=dc, es=es, bf=bf:
                          e.matmul(
                              UBK[dc // 4][:, (dc % 4) * 128:(dc % 4 + 1) * 128],
                              xtm_s(bf, tt, dc * 128, 128),
                              es[:], start=(tt == 0 and dc % 4 == 0),
                              stop=(tt == 1 and dc % 4 == 3)),
                          inc="sT")
        M(f"scU,{t}", tu)
        v_group(t + 1, 1)
        p.wait("vector", "sT", tu)
        vus = None
        for half in range(2):
            vus = p.op("vector", lambda e, half=half: e.tensor_scalar(
                usc_v[:, half * 512:(half + 1) * 512], UBK[half][:],
                T["scc"][:, 0:1], None, op0=ALU.mult), inc="sV")
        M(f"scUe,{t}", vus)
        p.wait("vector", "sV", vus)
        vm = None
        for half in range(2):
            vm = p.op("vector", lambda e, half=half: e.tensor_tensor(
                mm_v[:, half * 512:(half + 1) * 512],
                mm_v[:, half * 512:(half + 1) * 512],
                usc_v[:, half * 512:(half + 1) * 512], op=ALU.add), inc="sV")
        M(f"scM,{t}", vm)

    xtm_half(0, 0)
    xtm_half(0, 1)
    v_group(0, 0)
    v_group(0, 1)
    for t in range(NCHUNK):
        scan_chunk(t)
        if t + 2 < NCHUNK:
            chunk_loads(t + 2)

    for s in range(NYS):
        p.wait("gpsimd", "qYS", M(f"scYq,{(s + 1) * (NCHUNK // NYS) - 1}"))
        p.op("gpsimd", lambda e, s=s: e.collective_compute(
            "AllGather", ALU.bypass, replica_groups=[list(range(NCORE))],
            ins=[dr["ycon"].ap()[s * YR:(s + 1) * YR, :].opt()],
            outs=[dr["yg"].ap()[s * NCORE * YR:(s + 1) * NCORE * YR, :].opt()]),
            inc="cc")

    for eng in ("tensor", "vector", "scalar", "sync"):
        p.wait(eng, "cc", 3 + NYS)
    p.wait("sync", "qXP", p.cnt("qXP"))
    p.wait("sync", "qXF", p.cnt("qXF"))
    p.wait("sync", "sA", p.cnt("sA"))
    rl = p.dma("sync", T["wbc"][:], dr["wbc"].ap(), "qInit")
    p.wait("vector", "qInit", rl)
    p.wait("scalar", "qInit", rl)
    p.wait("tensor", "qInit", rl)
    ffn_head2_loads(0)
    ffn_head2_loads(1)
    ffn_head(0, 2)
    ffn_body_ut(0, 2)
    for i in range(NT):
        if i + 2 < NT:
            ffn_head2_loads(i + 2)
        nx = (lambda i=i: ffn_head(i + 1, 2)) if i + 1 < NT else None
        ffn_body(i, 2, nxt=nx)

    if debug:
        p.dma("gpsimd", dr["agcp"].ap(), dr["agin"].ap(), "qDC")
        p.wait("gpsimd", "qDC", p.cnt("qDC"))
        p.dma("gpsimd", dr["yccp"].ap(), dr["ycon"].ap(), "qDC")
        p.wait("gpsimd", "qDC", p.cnt("qDC"))
    for q in ("qSX0", "qSX1", "qGP", "qX1S"):
        p.wait(eng="gpsimd", semname=q, val=p.cnt(q))

    with nc.Block() as block:

        @block.tensor
        def _(eng):
            for f in p.streams["tensor"]:
                f(eng, None)

        @block.vector
        def _(eng):
            for f in p.streams["vector"]:
                f(eng, None)

        @block.scalar
        def _(eng):
            for f in p.streams["scalar"]:
                f(eng, None)

        @block.sync
        def _(eng):
            for f in p.streams["sync"]:
                f(eng, None)

        @block.gpsimd
        def _(eng):
            with eng.register("ry") as ry:
                regs = {"ry": ry}
                for f in p.streams["gpsimd"]:
                    f(eng, regs)

    stack.close()
    return nc


def _host_inputs(L0_attn_x, L0_attn_W0, L0_attn_Wkv, L0_attn_bkv, L0_attn_eta,
                 L0_ffn_W_fc, L0_ffn_ln_w, L0_ffn_ln_b, L0_ffn_W_proj):
    f32 = np.float32
    xr_full = np.ascontiguousarray(
        np.transpose(np.asarray(L0_attn_x, f32), (1, 0, 2)).reshape(NTOK, D))
    wfcT = np.ascontiguousarray(np.asarray(L0_ffn_W_fc, f32).T)
    wpT = np.ascontiguousarray(np.asarray(L0_ffn_W_proj, f32).T)
    W0 = np.asarray(L0_attn_W0, f32)
    Wkv = np.asarray(L0_attn_Wkv, f32)
    bkv = np.asarray(L0_attn_bkv, f32)
    ln_w = np.asarray(L0_ffn_ln_w, f32)
    ln_b = np.asarray(L0_ffn_ln_b, f32)
    eta = np.asarray(L0_attn_eta, f32).reshape(1, 1)

    idf = np.eye(128, dtype=f32)
    idb = idf.astype(ml_dtypes.bfloat16)
    ii = np.arange(128)
    maskl = -((ii[:, None] // 2) > (ii[None, :] // 2)).astype(f32)
    masku = np.ascontiguousarray(maskl.T)
    consts = np.zeros((128, 4), f32)
    consts[:, 0] = EPS_LN
    ones2 = np.ones((2, 128), f32).astype(ml_dtypes.bfloat16)
    wbc = np.ascontiguousarray(np.broadcast_to(ln_w[None, :], (128, H)))
    bcol = np.ascontiguousarray(ln_b.reshape(HCH, 128).T)
    wcol = np.ascontiguousarray(ln_w.reshape(HCH, 128).T)

    in_maps = []
    for k in range(NCORE):
        cols = slice(128 * k, 128 * (k + 1))
        yoff = np.zeros((NT, 1), np.uint32)
        for i in range(NT):
            s = (k * TPC) // YR
            rin = (k * TPC) % YR + i * 128
            yoff[i, 0] = (s * NCORE * YR + rin) * 128
        wslab = np.concatenate([
            wfcT[128 * k:128 * (k + 1), :],
            wpT[512 * k:512 * (k + 1), :].reshape(4, 128, D).transpose(
                1, 0, 2).reshape(128, H)]).astype(np.float32)
        in_maps.append({
            "xr": xr_full[k * TPC:(k + 1) * TPC],
            "wfcT": wfcT, "wpT": wpT,
            "w0T": np.ascontiguousarray(W0[cols, :].T),
            "wvT": np.ascontiguousarray(Wkv[D:2 * D][cols, :].T),
            "bvbc": np.ascontiguousarray(
                np.broadcast_to(bkv[D:2 * D][cols][None, :], (128, 128))),
            "wbc": wbc, "bcol": bcol, "wcol": wcol, "eta": eta,
            "idf": idf, "idb": idb, "maskl": maskl, "masku": masku,
            "consts": consts, "yoff": yoff, "wslab": np.ascontiguousarray(wslab),
        })
    return in_maps


def kernel(L0_attn_x, L0_attn_W0, L0_attn_Wkv, L0_attn_bkv, L0_attn_eta,
           L0_ffn_W_fc, L0_ffn_ln_w, L0_ffn_ln_b, L0_ffn_W_proj):
    in_maps = _host_inputs(L0_attn_x, L0_attn_W0, L0_attn_Wkv, L0_attn_bkv,
                           L0_attn_eta, L0_ffn_W_fc, L0_ffn_ln_w, L0_ffn_ln_b,
                           L0_ffn_W_proj)
    nc = build_program()
    res = run_bass_kernel_spmd(nc, in_maps, core_ids=list(range(NCORE)))
    outs = [np.asarray(res.results[k]["out"]) for k in range(NCORE)]
    full = np.concatenate(outs, axis=0)
    return np.ascontiguousarray(
        full.reshape(S, B, D).transpose(1, 0, 2)).astype(np.float32)


# version 19
# speedup vs baseline: 1.0979x; 1.0979x over previous
import numpy as np
import ml_dtypes
from contextlib import ExitStack
import concourse.bass as bass
import concourse.mybir as mybir
from concourse.bass_utils import run_bass_kernel_spmd

F32 = mybir.dt.float32
BF16 = mybir.dt.bfloat16
FP8 = mybir.dt.float8e4
U32 = mybir.dt.uint32
AF = mybir.ActivationFunctionType
ALU = mybir.AluOpType
AX = mybir.AxisListType

NCORE = 8
B, S, D, H = 2, 8192, 1024, 4096
NTOK = B * S
TPC = NTOK // NCORE
NT = TPC // 128
DCH = D // 128
HCH = H // 32 // 4
HCH = H // 128
HH = H // 2
CH = 256
NCHUNK = NTOK // CH
CPC = NCHUNK // NCORE
EPS_LN = 1e-5
EPS_ABS = 1e-6

XFM_R0 = 0
GT_R0 = 2048
PKG_R = GT_R0 + CPC * 3 * 16


class Plan:
    ENGINES = ("tensor", "vector", "scalar", "sync", "gpsimd")

    def __init__(self):
        self.streams = {e: [] for e in self.ENGINES}
        self.sems = {}

    def sem(self, name):
        if name not in self.sems:
            self.sems[name] = [None, 0]

    def cnt(self, name):
        return self.sems[name][1]

    def wait(self, eng, semname, val):
        if val is None or val <= 0:
            return
        assert val <= self.sems[semname][1], (
            f"{eng} waits {semname}>={val}, planned {self.sems[semname][1]}")
        s = self.sems[semname]
        self.streams[eng].append(lambda e, regs: e.wait_ge(s[0], val))

    def op(self, eng, fn, inc=None, incby=1):
        if inc is None:
            self.streams[eng].append(lambda e, regs: fn(e))
            return None
        s = self.sems[inc]
        s[1] += incby
        self.streams[eng].append(lambda e, regs: fn(e).then_inc(s[0], incby))
        return s[1]

    def opr(self, eng, fn, inc, incby=1):
        s = self.sems[inc]
        s[1] += incby
        self.streams[eng].append(
            lambda e, regs: fn(e, regs).then_inc(s[0], incby))
        return s[1]

    def selfw(self, eng):
        sem = {"tensor": "sT", "vector": "sV", "scalar": "sA"}[eng]
        self.wait(eng, sem, self.cnt(sem))

    def dma(self, eng, out_ap, in_ap, inc):
        return self.op(eng, lambda e: e.dma_start(out=out_ap, in_=in_ap),
                       inc=inc, incby=16)


def build_program(debug=False):
    nc = bass.Bass("TRN2", target_bir_lowering=False)
    dr = {}

    def din(name, shape, dt=F32):
        dr[name] = nc.dram_tensor(name, shape, dt, kind="ExternalInput")

    def dint(name, shape, dt=F32):
        kind = "ExternalOutput" if debug else "Internal"
        dr[name] = nc.dram_tensor(name, shape, dt, kind=kind)

    din("xr", [TPC, D])
    din("wfcT", [D, H])
    din("wpT", [H, D])
    din("w0T", [D, 128])
    din("wvT", [D, 128])
    din("bvbc", [128, 128])
    din("wbc", [128, H])
    din("bcol", [128, HCH])
    din("wcol", [128, HCH])
    din("eta", [1, 1])
    din("idf", [128, 128])
    din("idb", [128, 128], BF16)
    din("maskl", [128, 128])
    din("masku", [128, 128])
    din("consts", [128, 4])
    din("yoff", [NT, 1], U32)
    din("wslab", [256, H])

    dint("xmid", [TPC, D])
    dr["gin"] = nc.dram_tensor("gin", [1, 2], F32)
    dr["gout"] = nc.dram_tensor("gout_d", [1, 2], F32)
    dr["wqin"] = nc.dram_tensor("wqin", [256, H], FP8)
    dr["wqout"] = nc.dram_tensor("wqout", [NCORE * 256, H], FP8)
    dr["agin"] = nc.dram_tensor("agin", [PKG_R, D], F32)
    dr["agout"] = nc.dram_tensor("agout", [NCORE * PKG_R, D], F32,
                                 addr_space="Shared")
    dr["ycon"] = nc.dram_tensor("ycon", [NTOK, 128], F32)
    dr["yg"] = nc.dram_tensor("yg", [NCORE * NTOK, 128], F32,
                              addr_space="Shared")
    if debug:
        dr["dbgt"] = nc.dram_tensor("dbgt", [6, 128, H], F32,
                                    kind="ExternalOutput")
        dr["agcp"] = nc.dram_tensor("agcp", [PKG_R, D], F32,
                                    kind="ExternalOutput")
        dr["yccp"] = nc.dram_tensor("yccp", [NTOK, 128], F32,
                                    kind="ExternalOutput")
    dr["out"] = nc.dram_tensor("out", [TPC, D], F32, kind="ExternalOutput")

    agin_h = dr["agin"].ap().tensor
    agout_h = dr["agout"].ap().tensor
    yg_h = dr["yg"].ap().tensor

    p = Plan()
    DMASEMS = ("qInit", "qWsl", "qX0", "qX1", "qX1S", "qSX0", "qSX1",
               "qXP", "qXF", "qGT", "qSC0", "qSC1", "qYS", "qGP", "qDB", "qDC")
    for s in ("sT", "sV", "sA", "cc") + DMASEMS:
        p.sem(s)

    stack = ExitStack()
    T = {}

    def sba(name, shape, dt=F32):
        T[name] = stack.enter_context(nc.sbuf_tensor("sb_" + name, shape, dt))

    def psa(name, shape, dt=F32):
        T[name] = stack.enter_context(nc.psum_tensor(name, shape, dt))

    sba("wqfc", [128, DCH, H], FP8)
    sba("wqp", [128, HCH, D], FP8)
    sba("qacc", [128, 32])
    sba("onescol", [128, 1])
    sba("onesrow", [1, 128])
    sba("gam", [128, 2])
    sba("gam2", [128, 2])
    sba("consts", [128, 4])
    sba("idf", [128, 128])
    sba("idb", [128, 128], BF16)
    sba("maskl", [128, 128])
    sba("masku", [128, 128])
    sba("wbc", [128, H])
    sba("bcol", [128, HCH])
    sba("wcol", [128, HCH])
    sba("wc4", [128, HCH * 4], BF16)
    sba("crow4", [4, D])
    sba("r1p", [128, 4])
    sba("hws", [128, 1024])
    sba("gpart", [1, 2])
    sba("r1t", [4, 128])
    sba("eta", [1, 1])
    sba("scc", [128, 2])
    sba("stats", [128, 32])
    sba("statx", [128, 16])
    sba("wvT", [128, DCH * 128])
    sba("bvbc", [128, 128])
    sba("yofft", [NT, 1], U32)
    sba("x0a", [128, D])
    sba("x0b", [128, D])
    sba("u", [128, D])
    sba("upair", [128, 2 * D], BF16)
    sba("uT", [128, 2 * DCH * 128], BF16)
    sba("hg", [128, H])
    sba("hpairh", [128, H], BF16)
    sba("hTh", [128, HCH * 128], BF16)
    sba("gout", [128, D])
    sba("xout", [128, D])
    sba("sq8", [128, 12])
    sba("xln", [128, D])
    sba("xct", [128, 2 * DCH * 128])
    sba("lad", [128, 12 * 128])
    sba("xlt", [128, DCH * 128])
    sba("gt01", [128, 128])

    for i in range(8):
        psa(f"ps{i}", [128, 512], F32)

    for sname in ("sT", "sV", "sA", "cc") + DMASEMS:
        p.sems[sname][0] = stack.enter_context(nc.semaphore(sname))

    PS = [T[f"ps{i}"] for i in range(8)]

    wsl = T["hg"][:]

    sgn1 = T["hpairh"][:]
    sgn2 = T["hTh"][:]

    usc_v = T["xout"][:]
    wq5_v = T["upair"][:].bitcast(FP8)

    def xfm_s(bf, dc, t0, n):
        o = bf * 2048 + dc * 256 + t0
        return T["hg"][:, o:o + n]

    def xfm_dst(bf):
        return T["hg"][:, bf * 2048:(bf + 1) * 2048].rearrange(
            "p (c t) -> p c t", c=DCH)

    def xtm_s(bf, tt, c0, n):
        o = bf * 2048 + tt * 1024 + c0
        return T["wbc"][:, o:o + n]

    def xtm_dst(bf):
        return T["wbc"][:, bf * 2048:(bf + 1) * 2048].rearrange(
            "p (a q) -> p a q", a=2)

    def gts_s(bf, g):
        o = bf * 384 + g * 128
        return T["u"][:, o:o + 128]

    def gts_dst(bf):
        return T["u"][:, bf * 384:bf * 384 + 384].rearrange(
            "p (g q) -> p g q", g=3)

    
    x1s_v = T["xln"][:]
    xlt_v = T["xlt"][:]
    ygt_v = T["xct"][:, 0:1024]
    gt01_v = T["gt01"][:]
    mm_v = T["gout"][:]
    w0T_v = T["upair"][:].bitcast(F32)[:, 0:1024]
    ys_v = T["hpairh"][:, 0:512].bitcast(F32)
    vs_v = T["hpairh"][:, 512:1536].bitcast(F32)
    pe_v = T["hTh"][:, 0:1024].bitcast(F32)
    p0s_v = pe_v[:, 0:128]
    p1s_v = pe_v[:, 128:256]
    e0s_v = pe_v[:, 256:384]
    e1s_v = pe_v[:, 384:512]

    marks = {}

    def M(key, val=None):
        if val is None:
            return marks.get(key, 0)
        marks[key] = val
        return val

    for nm in ("consts", "idf", "idb", "maskl", "masku", "wbc", "bcol",
               "wcol", "eta", "bvbc"):
        p.dma("sync", T[nm][:], dr[nm].ap(), "qInit")
    p.dma("sync", T["yofft"][:], dr["yoff"].ap(), "qInit")
    p.dma("sync", T["wvT"][:].rearrange("p (c m) -> p c m", c=DCH),
          dr["wvT"].ap().rearrange("(c p) m -> p c m", p=128), "qInit")
    ld0 = p.cnt("qInit")

    p.op("vector", lambda e: e.memset(T["onescol"][:], 1.0), inc="sV")
    p.op("vector", lambda e: e.memset(T["onesrow"][:], 1.0), inc="sV")
    p.wait("vector", "qInit", ld0)
    p.op("vector", lambda e: e.tensor_scalar(
        T["scc"][0:1, 0:1], T["eta"][:], -2.0 / (B * D), None,
        op0=ALU.mult), inc="sV")
    p.wait("scalar", "qInit", ld0)
    p.op("scalar", lambda e: e.activation(
        T["scc"][0:1, 1:2], T["eta"][:], AF.Sqrt, scale=2.0 / (B * D)),
        inc="sA")
    vl, al = p.cnt("sV"), p.cnt("sA")
    p.wait("tensor", "sV", vl)
    p.wait("tensor", "sA", al)
    p.op("tensor", lambda e: e.matmul(
        PS[7][:, 0:2], T["onesrow"][:], T["scc"][0:1, 0:2],
        start=True, stop=True), inc="sT")
    tl = p.cnt("sT")
    p.wait("vector", "sT", tl)
    p.op("vector", lambda e: e.tensor_copy(T["scc"][:, 0:2], PS[7][:, 0:2]),
         inc="sV")

    p.wait("scalar", "sV", p.cnt("sV"))
    p.wait("sync", "sA", p.cnt("sA"))
    a_abs = {}
    SL = [("wfcT", 0, H)] + [("wpT", s, D) for s in range(4)]
    def slab_src(which, s):
        if which == "wfcT":
            return dr["wslab"].ap()[0:128, :]
        return dr["wslab"].ap()[128:256, :].rearrange(
            "p (q c) -> p q c", q=4)[:, s, :]
    for si, (which, s, cols) in enumerate(SL):
        p.wait("sync", "sA", a_abs.get(si - 1, 0))
        ld = p.dma("sync", wsl[:, 0:cols], slab_src(which, s), "qWsl")
        p.wait("scalar", "qWsl", ld)
        a_abs[si] = p.op("scalar", lambda e, si=si, n=cols: e.activation(
            sgn1[:, 0:n], wsl[:, 0:n], AF.Abs,
            accum_out=T["qacc"][:, si:si + 1]), inc="sA")
    p.wait("vector", "sA", p.cnt("sA"))
    p.op("vector", lambda e: e.tensor_reduce(
        T["stats"][:, 16:17], T["qacc"][:, 0:1], axis=AX.X, op=ALU.add),
        inc="sV")
    p.op("vector", lambda e: e.tensor_reduce(
        T["stats"][:, 17:18], T["qacc"][:, 1:5], axis=AX.X, op=ALU.add),
        inc="sV")
    p.wait("tensor", "sV", p.cnt("sV"))
    p.op("tensor", lambda e: e.matmul(
        PS[7][0:1, 4:5], T["stats"][:, 16:17], T["onescol"][:],
        start=True, stop=True), inc="sT")
    p.op("tensor", lambda e: e.matmul(
        PS[7][0:1, 5:6], T["stats"][:, 17:18], T["onescol"][:],
        start=False, stop=True, skip_group_check=True), inc="sT")
    p.wait("vector", "sT", p.cnt("sT"))
    p.op("vector", lambda e: e.tensor_copy(
        T["gpart"][0:1, 0:2], PS[7][0:1, 4:6]), inc="sV")
    vgp = p.cnt("sV")
    p.wait("gpsimd", "sV", vgp)
    p.dma("gpsimd", dr["gin"].ap(), T["gpart"][0:1, 0:2], "qDC")
    p.wait("gpsimd", "qDC", p.cnt("qDC"))
    p.op("gpsimd", lambda e: e.collective_compute(
        "AllReduce", ALU.add, replica_groups=[list(range(NCORE))],
        ins=[dr["gin"].ap().opt()], outs=[dr["gout"].ap().opt()]), inc="cc")
    p.wait("gpsimd", "cc", 1)
    ld = p.dma("gpsimd", T["gpart"][0:1, 0:2], dr["gout"].ap(), "qDC")
    p.wait("tensor", "qDC", ld)
    p.op("tensor", lambda e: e.matmul(
        PS[7][:, 4:6], T["onesrow"][:], T["gpart"][0:1, 0:2],
        start=True, stop=True), inc="sT")
    p.wait("vector", "sT", p.cnt("sT"))
    nel = float(D * H)
    p.op("vector", lambda e: e.tensor_scalar(
        T["gam"][:, 0:2], PS[7][:, 4:6], 0.5 / nel, 0.5 * EPS_ABS,
        op0=ALU.mult, op1=ALU.add), inc="sV")
    p.selfw("vector")
    p.op("vector", lambda e: e.tensor_scalar(
        T["gam2"][:, 0:2], T["gam"][:, 0:2], -1.0, None, op0=ALU.mult),
        inc="sV")
    p.wait("scalar", "sV", p.cnt("sV"))
    p.wait("sync", "sA", p.cnt("sA"))
    a_s2, v_q, q_st = {}, {}, {}
    for si, (which, s, cols) in enumerate(SL):
        p.wait("sync", "sA", a_s2.get(si - 1, 0))
        ld = p.dma("sync", wsl[:, 0:cols], slab_src(which, s), "qWsl")
        p.wait("scalar", "qWsl", ld)
        gcol = 0 if which == "wfcT" else 1
        p.wait("scalar", "sV", v_q.get(si - 2, 0))
        p.op("scalar", lambda e, n=cols, g=gcol: e.activation(
            sgn1[:, 0:n], wsl[:, 0:n], AF.Sign,
            bias=T["gam"][:, g:g + 1], scale=1.0), inc="sA")
        a_s2[si] = p.op("scalar", lambda e, n=cols, g=gcol: e.activation(
            sgn2[:, 0:n], wsl[:, 0:n], AF.Sign,
            bias=T["gam2"][:, g:g + 1], scale=1.0), inc="sA")
        p.wait("vector", "sA", a_s2[si])
        p.wait("vector", "qDB", q_st.get(si - 1, 0))
        v_q[si] = p.op("vector", lambda e, n=cols, si=si:
                       e.tensor_tensor(wq5_v[:, 0:n], sgn1[:, 0:n],
                                       sgn2[:, 0:n], op=ALU.add), inc="sV")
        p.wait("sync", "sV", v_q[si])
        if which == "wfcT":
            dst = dr["wqin"].ap()[0:128, :]
        else:
            dst = dr["wqin"].ap()[128:256, :].rearrange(
                "p (s c) -> p s c", s=4)[:, s, :]
        q_st[si] = p.dma("sync", dst, wq5_v[:, 0:cols], "qDB")
    p.wait("gpsimd", "qDB", p.cnt("qDB"))
    p.op("gpsimd", lambda e: e.collective_compute(
        "AllGather", ALU.bypass, replica_groups=[list(range(NCORE))],
        ins=[dr["wqin"].ap().opt()], outs=[dr["wqout"].ap().opt()]), inc="cc")
    p.wait("sync", "cc", 2)
    for dc in range(DCH):
        p.dma("sync", T["wqfc"][:, dc, :],
              dr["wqout"].ap()[dc * 256:dc * 256 + 128, :], "qWsl")
    for j in range(NCORE):
        p.dma("sync", T["wqp"][:, 4 * j:4 * (j + 1), :],
              dr["wqout"].ap()[j * 256 + 128:j * 256 + 256, :].rearrange(
                  "p (s c) -> p s c", s=4), "qWsl")
    wql = p.cnt("qWsl")
    p.wait("tensor", "qWsl", wql)
    p.wait("vector", "qWsl", wql)
    v_quant = p.cnt("sV")

    p.wait("vector", "qInit", ld0)
    p.selfw("vector")
    p.op("vector", lambda e: e.tensor_copy(
        T["wc4"][:, 0:4 * HCH:4], T["wcol"][:]), inc="sV")
    p.op("vector", lambda e: e.tensor_copy(
        T["wc4"][:, 2:4 * HCH:4], T["bcol"][:]), inc="sV")
    p.selfw("vector")
    p.op("vector", lambda e: e.tensor_tensor(
        T["wc4"][:, 1:4 * HCH:4], T["wcol"][:], T["wc4"][:, 0:4 * HCH:4],
        op=ALU.subtract), inc="sV")
    p.op("vector", lambda e: e.tensor_tensor(
        T["wc4"][:, 3:4 * HCH:4], T["bcol"][:], T["wc4"][:, 2:4 * HCH:4],
        op=ALU.subtract), inc="sV")
    vb = p.cnt("sV")
    p.wait("tensor", "sV", max(vb, v_quant))
    for q in range(2):
        for hc in range(HCH):
            p.op("tensor", lambda e, hc=hc, q=q: e.matmul(
                PS[5 + q][0:4, 0:512], T["wc4"][:, hc * 4:hc * 4 + 4],
                T["wqp"][:, hc, q * 512:(q + 1) * 512],
                start=(hc == 0), stop=(hc == HCH - 1)), inc="sT")
    tn = p.cnt("sT")
    p.wait("vector", "sT", tn)
    for q in range(2):
        p.op("vector", lambda e, q=q: e.tensor_copy(
            T["crow4"][0:4, q * 512:(q + 1) * 512], PS[5 + q][0:4, 0:512]),
            inc="sV")
    v_c0 = p.cnt("sV")

    def emit_ln_stats(st, sb, nfeat, half_rsd):
        p.selfw("vector")
        p.op("vector", lambda e: e.tensor_scalar(
            st[:, sb + 1:sb + 2], st[:, sb + 0:sb + 1], -1.0 / nfeat, None,
            op0=ALU.mult), inc="sV")
        p.selfw("vector")
        p.op("vector", lambda e: e.tensor_tensor(
            st[:, sb + 2:sb + 3], st[:, sb + 1:sb + 2], st[:, sb + 1:sb + 2],
            op=ALU.mult), inc="sV")
        p.op("vector", lambda e: e.tensor_scalar(
            st[:, sb + 3:sb + 4], st[:, sb + 7:sb + 8], 1.0 / nfeat, None,
            op0=ALU.mult), inc="sV")
        p.wait("vector", "sV", p.cnt("sV"))
        p.op("vector", lambda e: e.tensor_tensor(
            st[:, sb + 3:sb + 4], st[:, sb + 3:sb + 4], st[:, sb + 2:sb + 3],
            op=ALU.subtract), inc="sV")
        p.wait("scalar", "sV", p.cnt("sV"))
        p.op("scalar", lambda e: e.activation(
            st[:, sb + 4:sb + 5], st[:, sb + 3:sb + 4], AF.Sqrt,
            bias=T["consts"][:, 0:1], scale=1.0), inc="sA")
        p.wait("vector", "sA", p.cnt("sA"))
        p.op("vector", lambda e: e.reciprocal(
            st[:, sb + 5:sb + 6], st[:, sb + 4:sb + 5]), inc="sV")
        p.wait("vector", "sV", p.cnt("sV"))
        if half_rsd:
            p.op("vector", lambda e: e.tensor_scalar(
                st[:, sb + 6:sb + 7], st[:, sb + 5:sb + 6], 0.5, None,
                op0=ALU.mult), inc="sV")
        else:
            p.op("vector", lambda e: e.tensor_tensor(
                st[:, sb + 6:sb + 7], st[:, sb + 1:sb + 2],
                st[:, sb + 5:sb + 6], op=ALU.mult), inc="sV")
        return p.cnt("sV")

    def dbg_dump(slot, src_ap, ncols, eng="sync", waitsem=None, waitval=None):
        if not debug:
            return
        if waitsem is not None:
            p.wait(eng, waitsem, waitval)
        p.dma(eng, dr["dbgt"].ap()[slot, :, 0:ncols], src_ap, "qDB")
        p.wait(eng, "qDB", p.cnt("qDB"))
        p.wait("vector", "qDB", p.cnt("qDB"))
        p.wait("scalar", "qDB", p.cnt("qDB"))

    def ffn_head2_loads(i):
        par = i % 2
        x1sl = x1s_v if par == 0 else xlt_v
        ygtl = T["xct"][:, par * 1024:(par + 1) * 1024]
        qSX = "qSX0" if par == 0 else "qSX1"
        p.wait("sync", "sV", M(f"x2a,{i-2}"))
        p.wait("sync", "qX1S", M(f"ldx1,{i-1}"))
        p.wait("sync", qSX, M(f"stx1,{i}"))
        ldx1 = p.dma("sync", x1sl,
                     dr["xmid"].ap()[i * 128:(i + 1) * 128, :], "qX1S")
        M(f"ldx1,{i}", ldx1)
        p.wait("gpsimd", "cc", 4)
        p.wait("gpsimd", "sV", M(f"x2a,{i-2}"))
        p.wait("gpsimd", "qGP", M(f"ldy,{i-1}"))

        def g_ygt(e, regs):
            e.reg_load(regs["ry"], T["yofft"][i:i + 1, 0:1])
            return e.dma_start(
                out=ygtl.rearrange("p (a b) -> p a b", a=NCORE),
                in_=bass.AP(yg_h, regs["ry"],
                            [[128, 128], [NTOK * 128, NCORE], [1, 128]]))
        ldy = p.opr("gpsimd", g_ygt, inc="qGP", incby=16)
        M(f"ldy,{i}", ldy)

    def ffn_head(i, phase):
        par = i % 2
        sb = 8 * par
        sb2 = 16 + 8 * par
        x0 = T["x0a"] if par == 0 else T["x0b"]
        st = T["stats"]

        qX = "qX0" if par == 0 else "qX1"
        qSX = "qSX0" if par == 0 else "qSX1"
        if phase == 1:
            p.wait("sync", "sV", M(f"res{phase},{i-2}"))
            ldx = p.dma("sync", x0[:], dr["xr"].ap()[i * 128:(i + 1) * 128, :],
                        qX)
            p.wait("vector", qX, ldx)
            p.wait("scalar", qX, ldx)
        else:
            x1sl = x1s_v if par == 0 else xlt_v
            ygtl = T["xct"][:, par * 1024:(par + 1) * 1024]
            p.wait("vector", "qX1S", M(f"ldx1,{i}"))
            p.wait("vector", "qGP", M(f"ldy,{i}"))
            p.wait("vector", "sV", M(f"res{phase},{i-2}"))
            vx2 = p.op("vector", lambda e: e.tensor_tensor(
                x0[:], x1sl,
                ygtl, op=ALU.add),
                inc="sV")
            M(f"x2a,{i}", vx2)
            p.wait("scalar", "sV", vx2)

        p.selfw("vector")
        p.op("vector", lambda e: e.tensor_reduce(
            st[:, sb + 0:sb + 1], x0[:], axis=AX.X, op=ALU.add), inc="sV")
        p.wait("scalar", "sV", M(f"upair,{i-1}"))
        a_sq1 = p.op("scalar", lambda e: e.activation(
            T["u"][:], x0[:], AF.Square, accum_out=st[:, sb + 7:sb + 8]),
            inc="sA")
        p.wait("vector", "sA", a_sq1)
        v_st1 = emit_ln_stats(st, sb, D, half_rsd=True)
        p.op("vector", lambda e: e.tensor_scalar(
            T["u"][:], x0[:], st[:, sb + 1:sb + 2], None, op0=ALU.add),
            inc="sV")
        p.selfw("vector")
        p.wait("vector", "sT", M(f"uTt,{i-1}"))
        p.op("vector", lambda e: e.tensor_copy(
            T["upair"][:, 0:D], T["u"][:]), inc="sV")
        p.wait("vector", "sV", p.cnt("sV"))
        vup = p.op("vector", lambda e: e.tensor_tensor(
            T["upair"][:, D:2 * D], T["u"][:], T["upair"][:, 0:D],
            op=ALU.subtract), inc="sV")
        M(f"upair,{i}", vup)

        if debug and phase == 1 and i == 0:
            dbg_dump(0, x0[:], D, waitsem="sV", waitval=p.cnt("sV"))
            dbg_dump(1, T["u"][:], D)
            dbg_dump(2, T["stats"][:, 0:8], 8)
        M(f"vst1,{i}", v_st1)

    def ffn_body_ut(i, phase):
        par = i % 2
        sb = 8 * par
        sb2 = 16 + 8 * par
        x0 = T["x0a"] if par == 0 else T["x0b"]
        st = T["stats"]
        qSX = "qSX0" if par == 0 else "qSX1"
        v_st1 = M(f"vst1,{i}")
        vup = M(f"upair,{i}")
        p.wait("tensor", "sV", vup)
        p.wait("tensor", "sV", max(M("b45free"), M(f"hTev,{i-1},6"),
                                   M(f"hTev,{i-1},7")))
        tt = None
        for b in range(16):
            bank, slot = 4 + b // 8, b % 8
            tt = p.op("tensor", lambda e, b=b, bank=bank, slot=slot:
                      e.transpose(
                          PS[bank][:].bitcast(BF16)[:,
                                                    slot * 128:(slot + 1) * 128],
                          T["upair"][:, b * 128:(b + 1) * 128], T["idb"][:]),
                      inc="sT")
        M(f"uTt,{i}", tt)
        p.wait("vector", "sT", tt)
        ve = None
        for b45 in range(2):
            ve = p.op("vector", lambda e, b45=b45: e.tensor_copy(
                T["uT"][:, b45 * 1024:(b45 + 1) * 1024],
                PS[4 + b45][:].bitcast(BF16)), inc="sV")
        M("b45free", ve)
        M(f"uTe,{i}", ve)

    def fc_group(i, pc, phase):
        bank = pc % 2
        if pc >= 2:
            p.wait("tensor", "sA", M(f"gelu,{i},{pc-2}"))
        else:
            p.wait("tensor", "sA", M(f"gelu,{i-1},{pc+6}"))
        for dc in range(DCH):
            for hl in range(2):
                p.op("tensor",
                     lambda e, pc=pc, dc=dc, hl=hl, bank=bank:
                     e.matmul(
                         PS[bank][:],
                         T["uT"][:, (hl * DCH + dc) * 128:
                                 (hl * DCH + dc + 1) * 128],
                         T["wqfc"][:, dc, pc * 512:(pc + 1) * 512],
                         start=(dc == 0 and hl == 0),
                         stop=(dc == DCH - 1 and hl == 1)),
                     inc="sT")
        return M(f"tfc{phase},{i},{pc}", p.cnt("sT"))

    def ffn_body(i, phase, nxt=None):
        par = i % 2
        sb = 8 * par
        sb2 = 16 + 8 * par
        x0 = T["x0a"] if par == 0 else T["x0b"]
        st = T["stats"]
        qSX = "qSX0" if par == 0 else "qSX1"
        v_st1 = M(f"vst1,{i}")
        ve = M(f"uTe,{i}")
        if i == 0:
            p.selfw("scalar")
        if phase == 1 and i == 0:
            p.wait("tensor", "sV", v_c0)
        p.wait("tensor", "sV", ve)
        p.wait("scalar", "sV", v_st1)
        for pc in range(10):
            if pc < 8:
                if f"tfc{phase},{i},{pc}" in marks:
                    tfc = M(f"tfc{phase},{i},{pc}")
                else:
                    tfc = fc_group(i, pc, phase)
                bank = pc % 2
                p.wait("scalar", "sT", tfc)
                if pc == 0:
                    p.selfw("scalar")
                    p.wait("scalar", "sV", M(f"sg2add,{i-1}"))
                hwgate = (M(f"hwm,{i},{pc-8}") if pc >= 8 else
                          M(f"hwm,{i-1},{pc}"))
                p.wait("scalar", "sV", hwgate)
                ag = p.op("scalar", lambda e, pc=pc, bank=bank: e.activation(
                    T["hg"][:, pc * 512:(pc + 1) * 512], PS[bank][:], AF.Gelu,
                    scale=st[:, sb + 6:sb + 7],
                    accum_out=T["sq8"][:, pc:pc + 1]), inc="sA")
                M(f"gelu,{i},{pc}", ag)
                sl = pc % 4
                uo = (pc % 2) * 512
                p.wait("vector", "sA", ag)
                p.selfw("vector")
                vw = p.op("vector", lambda e, pc=pc, uo=uo: e.tensor_tensor(
                    T["hws"][:, uo:uo + 512], T["hg"][:, pc * 512:(pc + 1) * 512],
                    T["wbc"][:, pc * 512:(pc + 1) * 512], op=ALU.mult),
                    inc="sV")
                M(f"hwm,{i},{pc}", vw)
                spgate = (M(f"hTt,{i},{pc-4}") if pc >= 4 else
                          M(f"hTt,{i-1},{pc+4}"))
                p.wait("vector", "sT", spgate)
                p.selfw("vector")
                p.op("vector", lambda e, sl=sl, uo=uo: e.tensor_copy(
                    T["hpairh"][:, sl * 1024:sl * 1024 + 512],
                    T["hws"][:, uo:uo + 512]), inc="sV")
                p.selfw("vector")
                vsp = p.op("vector", lambda e, sl=sl, uo=uo: e.tensor_tensor(
                    T["hpairh"][:, sl * 1024 + 512:sl * 1024 + 1024],
                    T["hws"][:, uo:uo + 512],
                    T["hpairh"][:, sl * 1024:sl * 1024 + 512],
                    op=ALU.subtract), inc="sV")
                M(f"hsplit,{i},{pc}", vsp)
                if pc % 2 == 1:
                    qt = pc // 2
                    if qt == 0:
                        p.wait("scalar", "sV", M(f"res{phase},{i-1}"))
                        p.wait("scalar", "sV", M(f"sg2r,{i-1}"))
                    p.selfw("scalar")
                    p.op("scalar", lambda e, qt=qt: e.activation(
                        T["gout"][:], T["hg"][:, qt * D:(qt + 1) * D],
                        AF.Square, accum_out=T["sq8"][:, 8 + qt:9 + qt]),
                        inc="sA")
                    M(f"sqq,{i},{qt}", p.cnt("sA"))
                if pc == 2 and nxt is not None:
                    nxt()
            if 1 <= pc <= 8:
                pt = pc - 1
                sl = pt % 4
                b45 = 4 + pt % 2
                p.wait("tensor", "sV", M(f"hsplit,{i},{pt}"))
                bgate = (M(f"hTev,{i},{pt-2}") if pt >= 2 else
                         max(M(f"uTe,{i}"), M(f"hTev,{i-1},{pt+6}")))
                p.wait("tensor", "sV", bgate)
                ttp = None
                for blk in range(8):
                    ttp = p.op("tensor",
                               lambda e, sl=sl, blk=blk, b45=b45:
                               e.transpose(
                                   PS[b45][:].bitcast(BF16)[
                                       :, blk * 128:(blk + 1) * 128],
                                   T["hpairh"][:, sl * 1024 + blk * 128:
                                               sl * 1024 + (blk + 1) * 128],
                                   T["idb"][:]), inc="sT")
                M(f"hTt,{i},{pt}", ttp)
                p.wait("vector", "sT", ttp)
                pg = (M(f"projg,{i},{pt-4}") if pt >= 4 else
                      M(f"projg,{i-1},{pt+4}"))
                p.wait("vector", "sT", pg)
                vep = p.op("vector", lambda e, sl=sl, b45=b45: e.tensor_copy(
                    T["hTh"][:, sl * 1024:(sl + 1) * 1024],
                    PS[b45][:].bitcast(BF16)), inc="sV")
                M(f"hTev,{i},{pt}", vep)
            if pc >= 2:
                pp = pc - 2
                sl = pp % 4
                p.wait("tensor", "sV", M(f"hTev,{i},{pp}"))
                if pp == 0:
                    p.wait("tensor", "sV", M(f"fev,{i-1}"))
                for q in range(2):
                    for hl in range(2):
                        for hcl in range(4):
                            tci = sl * 8 + hl * 4 + hcl
                            hcg = pp * 4 + hcl
                            p.op("tensor",
                                 lambda e, q=q, tci=tci, hcg=hcg, pp=pp,
                                 hl=hl, hcl=hcl:
                                 e.matmul(
                                     PS[2 + q][:],
                                     T["hTh"][:, tci * 128:(tci + 1) * 128],
                                     T["wqp"][:, hcg,
                                              q * 512:(q + 1) * 512],
                                     start=(pp == 0 and hl == 0 and hcl == 0),
                                     stop=False), inc="sT")
                M(f"projg,{i},{pp}", p.cnt("sT"))
        p.wait("vector", "sA", M(f"gelu,{i},7"))
        p.selfw("vector")
        vg1 = p.op("vector", lambda e: e.tensor_reduce(
            st[:, sb2 + 0:sb2 + 1], T["sq8"][:, 0:8], axis=AX.X, op=ALU.add),
            inc="sV")
        p.wait("vector", "sA", M(f"sqq,{i},3"))
        p.op("vector", lambda e: e.tensor_reduce(
            st[:, sb2 + 7:sb2 + 8], T["sq8"][:, 8:12], axis=AX.X, op=ALU.add),
            inc="sV")
        M(f"sg2r,{i}", p.cnt("sV"))
        M(f"sg2add,{i}", p.cnt("sV"))
        if debug and phase == 1 and i == 0:
            dbg_dump(3, T["hg"][:], H, waitsem="sA", waitval=p.cnt("sA"))
        v_st2 = emit_ln_stats(st, sb2, H, half_rsd=False)
        p.op("vector", lambda e: e.tensor_scalar(
            st[:, sb2 + 3:sb2 + 4], st[:, sb2 + 5:sb2 + 6], 0.25, None,
            op0=ALU.mult), inc="sV")
        p.selfw("vector")
        p.wait("vector", "sT", M("r1tt"))
        for c_ in range(2):
            p.op("vector", lambda e, c_=c_: e.tensor_copy(
                T["r1p"][:, c_:c_ + 1], st[:, sb2 + 1:sb2 + 2]), inc="sV")
            p.op("vector", lambda e, c_=c_: e.tensor_copy(
                T["r1p"][:, 2 + c_:3 + c_], st[:, sb2 + 4:sb2 + 5]), inc="sV")
        vr1 = p.cnt("sV")
        if i + 1 < NT:
            ffn_body_ut(i + 1, phase)
            p.wait("tensor", "sV", M(f"uTe,{i+1}"))
            fc_group(i + 1, 0, phase)
        p.wait("tensor", "sV", vr1)
        p.wait("tensor", "sV", M("b6free"))
        tr1 = p.op("tensor", lambda e: e.transpose(
            PS[6][0:4, 0:128], T["r1p"][:], T["idf"][:]), inc="sT")
        M("r1tt", tr1)
        if i + 1 < NT:
            fc_group(i + 1, 1, phase)
        p.wait("vector", "sT", tr1)
        vr1e = p.op("vector", lambda e: e.tensor_copy(
            T["r1t"][0:4, 0:128], PS[6][0:4, 0:128]), inc="sV")
        M("b6free", vr1e)
        p.wait("tensor", "sV", vr1e)
        tc0 = None
        for q in range(2):
            tc0 = p.op("tensor", lambda e, q=q: e.matmul(
                PS[2 + q][:], T["r1t"][0:4, 0:128],
                T["crow4"][0:4, q * 512:(q + 1) * 512], start=False,
                stop=True), inc="sT")
        p.wait("vector", "sT", tc0)
        p.selfw("vector")
        af = None
        for q in range(2):
            af = p.op("vector", lambda e, q=q: e.tensor_scalar(
                T["gout"][:, q * 512:(q + 1) * 512], PS[2 + q][:],
                st[:, sb2 + 3:sb2 + 4], None, op0=ALU.mult), inc="sV")
        M(f"fev,{i}", af)
        if debug and phase == 1 and i == 0:
            dbg_dump(5, T["gout"][:], D, waitsem="sV", waitval=p.cnt("sV"))
        qSXo = "qSX1" if par == 0 else "qSX0"
        p.wait("vector", qSXo, M(f"stx{phase},{i-1}"))
        p.selfw("vector")
        vres = p.op("vector", lambda e: e.tensor_tensor(
            T["xout"][:], x0[:], T["gout"][:], op=ALU.add), inc="sV")
        M(f"res{phase},{i}", vres)
        p.wait("sync", "sV", vres)
        dst = dr["xmid"] if phase == 1 else dr["out"]
        stx = p.dma("sync", dst.ap()[i * 128:(i + 1) * 128, :], T["xout"][:],
                    qSX)
        M(f"stx{phase},{i}", stx)
        return vres

    def x_epilogue(i):
        par = i % 2
        sbx = 8 * par
        st = T["statx"]
        vres = M(f"res1,{i}")
        p.wait("vector", "sV", vres)
        p.op("vector", lambda e: e.tensor_reduce(
            st[:, sbx + 0:sbx + 1], T["xout"][:], axis=AX.X, op=ALU.add),
            inc="sV")
        p.wait("scalar", "sV", M(f"xlnw,{i-2}"))
        p.wait("scalar", "sV", vres)
        a_sq = p.op("scalar", lambda e: e.activation(
            T["xln"][:], T["xout"][:], AF.Square,
            accum_out=st[:, sbx + 7:sbx + 8]), inc="sA")
        p.wait("vector", "sA", a_sq)
        p.selfw("vector")
        p.op("vector", lambda e: e.tensor_scalar(
            st[:, sbx + 1:sbx + 2], st[:, sbx + 0:sbx + 1], -1.0 / D, None,
            op0=ALU.mult), inc="sV")
        p.selfw("vector")
        p.op("vector", lambda e: e.tensor_tensor(
            st[:, sbx + 2:sbx + 3], st[:, sbx + 1:sbx + 2],
            st[:, sbx + 1:sbx + 2], op=ALU.mult), inc="sV")
        p.op("vector", lambda e: e.tensor_scalar(
            st[:, sbx + 3:sbx + 4], st[:, sbx + 7:sbx + 8], 1.0 / D, None,
            op0=ALU.mult), inc="sV")
        p.wait("vector", "sV", p.cnt("sV"))
        p.op("vector", lambda e: e.tensor_tensor(
            st[:, sbx + 3:sbx + 4], st[:, sbx + 3:sbx + 4],
            st[:, sbx + 2:sbx + 3], op=ALU.subtract), inc="sV")
        p.wait("scalar", "sV", p.cnt("sV"))
        p.op("scalar", lambda e: e.activation(
            st[:, sbx + 4:sbx + 5], st[:, sbx + 3:sbx + 4], AF.Sqrt,
            bias=T["consts"][:, 0:1], scale=1.0), inc="sA")
        p.wait("vector", "sA", p.cnt("sA"))
        p.op("vector", lambda e: e.reciprocal(
            st[:, sbx + 5:sbx + 6], st[:, sbx + 4:sbx + 5]), inc="sV")
        p.wait("vector", "sV", p.cnt("sV"))
        p.op("vector", lambda e: e.tensor_tensor(
            st[:, sbx + 6:sbx + 7], st[:, sbx + 1:sbx + 2],
            st[:, sbx + 5:sbx + 6], op=ALU.mult), inc="sV")
        p.wait("vector", "sV", p.cnt("sV"))
        vx = p.op("vector", lambda e: e.tensor_scalar(
            T["xln"][:], T["xout"][:], st[:, sbx + 5:sbx + 6],
            st[:, sbx + 6:sbx + 7], op0=ALU.mult, op1=ALU.add), inc="sV")
        M(f"xlnw,{i}", vx)
        M(f"xtms,{i}", 0)
        p.wait("tensor", "sV", vx)
        p.wait("vector", "qXF", M(f"xlts,{i-1}"))
        ve = None
        for rnd in range(2):
            p.wait("tensor", "sV", M("b6free"))
            for b in range(4):
                dc = rnd * 4 + b
                tt = p.op("tensor", lambda e, dc=dc, b=b: e.transpose(
                    PS[6][:, b * 128:(b + 1) * 128],
                    T["xln"][:, dc * 128:(dc + 1) * 128], T["idf"][:]),
                    inc="sT")
            p.wait("vector", "sT", tt)
            ve = p.op("vector", lambda e, rnd=rnd: e.tensor_copy(
                xlt_v[:, rnd * 512:(rnd + 1) * 512], PS[6][:]), inc="sV")
            M("b6free", ve)
        p.wait("sync", "sV", ve)
        stf = p.dma("sync", bass.AP(agin_h, XFM_R0 * D + i * 128,
                                    [[TPC, 128], [128 * TPC, DCH], [1, 128]]),
                    xlt_v.rearrange("p (c m) -> p c m", c=DCH), "qXF")
        M(f"xlts,{i}", stf)
        p.wait("vector", "sV", M(f"gdone,{(i//2)-1}"))
        p.selfw("vector")
        vxc = p.op("vector", lambda e, par=par: e.tensor_scalar(
            T["xct"][:, par * 1024:(par + 1) * 1024], xlt_v,
            T["scc"][:, 1:2], None, op0=ALU.mult), inc="sV")
        M(f"xct,{i}", vxc)

    LSQ = {2: (1, 0), 3: (0, 1), 4: (3, 2), 5: (2, 3), 6: (5, 4),
           7: (4, 5), 8: (7, 6), 9: (6, 7), 10: (9, 8)}

    def gt_package(l):
        p.wait("tensor", "sV", M(f"xct,{2*l}"))
        p.wait("tensor", "sV", M(f"xct,{2*l+1}"))

        def gmm(bank, pa, pb):
            tg = None
            for dc in range(DCH):
                tg = p.op("tensor", lambda e, dc=dc, bank=bank, pa=pa, pb=pb:
                          e.matmul(PS[bank][:, 0:128],
                                   T["xct"][:, pa * 1024 + dc * 128:
                                             pa * 1024 + (dc + 1) * 128],
                                   T["xct"][:, pb * 1024 + dc * 128:
                                             pb * 1024 + (dc + 1) * 128],
                                   start=(dc == 0), stop=(dc == DCH - 1)),
                          inc="sT")
            return tg

        p.wait("tensor", "sV", M("b7free"))
        tg00 = gmm(7, 0, 0)
        p.wait("vector", "sT", tg00)
        p.wait("vector", "qGT", p.cnt("qGT"))
        p.op("vector", lambda e: e.tensor_tensor(
            T["lad"][:, 0:128], PS[7][:, 0:128], T["maskl"][:], op=ALU.mult),
            inc="sV")
        vN0 = p.op("vector", lambda e: e.tensor_tensor(
            T["lad"][:, 128:256], PS[7][:, 0:128], T["masku"][:], op=ALU.mult),
            inc="sV")
        M("b7free", vN0)

        def ladder(blk, bank):
            lad = T["lad"]
            p.selfw("vector")
            p.op("vector", lambda e: e.tensor_tensor(
                lad[:, 11 * 128:12 * 128], lad[:, 128:256], T["idf"][:],
                op=ALU.add), inc="sV")
            p.wait("tensor", "sV", p.cnt("sV"))
            for dstslot in range(2, 11):
                a, b = LSQ[dstslot]
                tq = p.op("tensor", lambda e, a=a, b=b, bank=bank: e.matmul(
                    PS[bank][:, 0:128], T["lad"][:, a * 128:(a + 1) * 128],
                    T["lad"][:, b * 128:(b + 1) * 128],
                    start=True, stop=True), inc="sT")
                p.wait("vector", "sT", tq)
                vq2 = p.op("vector", lambda e, d=dstslot, bank=bank:
                           e.tensor_copy(T["lad"][:, d * 128:(d + 1) * 128],
                                         PS[bank][:, 0:128]), inc="sV")
                p.wait("tensor", "sV", vq2)
            for nslot in (2, 4, 6, 8, 10):
                p.selfw("vector")
                p.op("tensor", lambda e, bank=bank: e.matmul(
                    PS[bank][:, 0:128], T["idf"][:],
                    T["lad"][:, 11 * 128:12 * 128],
                    start=True, stop=False), inc="sT")
                tq = p.op("tensor", lambda e, n=nslot, bank=bank: e.matmul(
                    PS[bank][:, 0:128], T["lad"][:, n * 128:(n + 1) * 128],
                    T["lad"][:, 11 * 128:12 * 128],
                    start=False, stop=True), inc="sT")
                p.wait("vector", "sT", tq)
                vq2 = p.op("vector", lambda e, bank=bank: e.tensor_copy(
                    T["lad"][:, 11 * 128:12 * 128], PS[bank][:, 0:128]),
                    inc="sV")
                p.wait("tensor", "sV", vq2)
            p.wait("sync", "sV", p.cnt("sV"))
            return p.dma("sync",
                         bass.AP(agin_h, (GT_R0 + l * 48 + blk * 16) * D,
                                 [[128, 128], [1, 128]]),
                         T["lad"][:, 11 * 128:12 * 128], "qGT")

        st0 = ladder(0, 7)
        M("b7free", p.cnt("sV"))
        p.wait("tensor", "sV", M("b6free"))
        tg11 = gmm(6, 1, 1)
        p.wait("vector", "sT", tg11)
        p.wait("vector", "qGT", st0)
        p.op("vector", lambda e: e.tensor_tensor(
            T["lad"][:, 0:128], PS[6][:, 0:128], T["maskl"][:], op=ALU.mult),
            inc="sV")
        vN1 = p.op("vector", lambda e: e.tensor_tensor(
            T["lad"][:, 128:256], PS[6][:, 0:128], T["masku"][:], op=ALU.mult),
            inc="sV")
        M("b6free", vN1)
        ladder(1, 6)
        M("b6free", p.cnt("sV"))
        p.wait("tensor", "sV", M("b7free"))
        tg01 = gmm(7, 0, 1)
        p.wait("vector", "sT", tg01)
        p.wait("vector", "qGT", p.cnt("qGT"))
        vg01 = p.op("vector", lambda e: e.tensor_scalar(
            gt01_v, PS[7][:, 0:128], -1.0, None, op0=ALU.mult),
            inc="sV")
        M("b7free", vg01)
        p.wait("sync", "sV", vg01)
        stg = p.dma("sync", bass.AP(agin_h, (GT_R0 + l * 48 + 32) * D,
                                    [[128, 128], [1, 128]]), gt01_v, "qGT")
        M(f"gtst,{l}", stg)
        M(f"gdone,{l}", p.cnt("sV"))

    ffn_head(0, 1)
    ffn_body_ut(0, 1)
    for i in range(NT):
        nx = (lambda i=i: ffn_head(i + 1, 1)) if i + 1 < NT else None
        ffn_body(i, 1, nxt=nx)
        x_epilogue(i)
        if i % 2 == 1:
            gt_package(i // 2)

    for q in ("qXF", "qGT"):
        p.wait("gpsimd", q, p.cnt(q))
    p.op("gpsimd", lambda e: e.collective_compute(
        "AllGather", ALU.bypass, replica_groups=[list(range(NCORE))],
        ins=[dr["agin"].ap().opt()], outs=[dr["agout"].ap().opt()]),
        inc="cc")

    p.wait("sync", "cc", 3)
    p.wait("vector", "cc", 3)
    p.wait("vector", "qSX0", p.cnt("qSX0"))
    p.wait("vector", "qSX1", p.cnt("qSX1"))
    ldw0 = p.dma("sync", w0T_v.rearrange("p (c m) -> p c m", c=DCH),
                 dr["w0T"].ap().rearrange("(c p) m -> p c m", p=128), "qInit")
    p.wait("vector", "qInit", ldw0)
    p.op("vector", lambda e: e.tensor_tensor(
        mm_v, w0T_v, T["wvT"][:], op=ALU.subtract), inc="sV")
    v_minit = p.cnt("sV")

    def chunk_loads(t):
        j, l = t // CPC, t % CPC
        bf = t % 2
        p.wait("sync", "sT", M(f"scE1,{t-2}"))
        qsc = "qSC0" if bf == 0 else "qSC1"
        base = j * PKG_R * D
        p.dma("sync", xfm_dst(bf),
              bass.AP(agout_h, base + XFM_R0 * D + l * CH,
                      [[TPC, 128], [128 * TPC, DCH], [1, CH]]), qsc)
        ldg = p.dma("sync", gts_dst(bf),
                    bass.AP(agout_h, base + (GT_R0 + l * 48) * D,
                            [[128, 128], [16 * D, 3], [1, 128]]), qsc)
        M(f"scld,{t}", ldg)

    chunk_loads(0)
    chunk_loads(1)

    VB = (PS[0][:, 0:128], PS[0][:, 128:256])
    PB = (PS[1][:, 0:128], PS[1][:, 128:256])
    EB = (PS[2][:, 0:128], PS[2][:, 128:256])
    UBK = (PS[3], PS[4])

    def xtm_half(u, tt):
        if u >= NCHUNK - 1:
            return
        bf = u % 2
        if tt == 0:
            p.wait("tensor", "qSC0" if bf == 0 else "qSC1", M(f"scld,{u}"))
        p.wait("tensor", "sA", M("xtmb56"))
        tq = None
        for dc in range(DCH):
            bank = 5 + dc // 4
            tq = p.op("tensor", lambda e, bf=bf, tt=tt, dc=dc, bank=bank:
                      e.transpose(
                          PS[bank][:, (dc % 4) * 128:(dc % 4 + 1) * 128],
                          xfm_s(bf, dc, tt * 128, 128), T["idf"][:]),
                      inc="sT")
        p.wait("scalar", "sT", tq)
        p.wait("scalar", "sT", M(f"scU,{u-2}"))
        ae = None
        for bk in range(2):
            ae = p.op("scalar", lambda e, bf=bf, tt=tt, bk=bk:
                      e.activation(
                          xtm_s(bf, tt, bk * 512, 512),
                          PS[5 + bk][:, 0:512], AF.Copy,
                          scale=T["scc"][:, 0:1]),
                      inc="sA")
        M("xtmb56", ae)
        if tt == 1:
            M(f"xtme,{u}", p.cnt("sA"))

    def v_group(t, tt):
        if t >= NCHUNK:
            return
        bf = t % 2
        if tt == 0:
            p.wait("tensor", "qSC0" if bf == 0 else "qSC1", M(f"scld,{t}"))
            p.wait("tensor", "sV", M(f"scVe,{t-1}"))
        tv = None
        for dc in range(DCH):
            tv = p.op("tensor", lambda e, tt=tt, dc=dc, bf=bf: e.matmul(
                VB[tt], xfm_s(bf, dc, tt * 128, 128),
                T["wvT"][:, dc * 128:(dc + 1) * 128],
                start=(dc == 0), stop=(dc == DCH - 1)), inc="sT")
        M(f"scV,{t},{tt}", tv)

    def scan_chunk(t):
        bf = t % 2
        vsl = vs_v[:, bf * 256:(bf + 1) * 256]
        p.wait("tensor", "qSC0" if bf == 0 else "qSC1", M(f"scld,{t}"))
        p.wait("tensor", "sV", M(f"scPe,{t-1}"))
        for dc in range(DCH):
            if dc == 0:
                p.wait("tensor", "sV",
                       M(f"scM,{t-1},0") if t > 0 else v_minit)
            if dc == 4:
                p.wait("tensor", "sV", M(f"scM,{t-1}") if t > 0 else v_minit)
            tp_ = p.op("tensor", lambda e, dc=dc, bf=bf: e.matmul(
                PB[0], xfm_s(bf, dc, 0, 128),
                mm_v[:, dc * 128:(dc + 1) * 128], start=(dc == 0),
                stop=(dc == DCH - 1)), inc="sT")
        M(f"scP,{t},0", tp_)
        xtm_half(t + 1, 0)
        p.wait("vector", "sT", M(f"scV,{t},1"))
        p.wait("vector", "qYS", M(f"scYq,{t-1}"))
        vv = None
        for tt in range(2):
            vv = p.op("vector", lambda e, tt=tt, vsl=vsl: e.tensor_tensor(
                vsl[:, tt * 128:(tt + 1) * 128], VB[tt],
                T["bvbc"][:], op=ALU.add), inc="sV")
        M(f"scVe,{t}", vv)
        p.wait("vector", "sT", M(f"scP,{t},0"))
        vp0 = p.op("vector", lambda e: e.tensor_tensor(
            p0s_v, PB[0], T["bvbc"][:], op=ALU.subtract),
            inc="sV")
        p.wait("tensor", "sV", vp0)
        p.wait("tensor", "sV", M(f"scEe,{t-1}"))
        te0 = p.op("tensor", lambda e, bf=bf: e.matmul(
            EB[0], gts_s(bf, 0), p0s_v, start=True,
            stop=True), inc="sT")
        for dc in range(DCH):
            tp_ = p.op("tensor", lambda e, dc=dc, bf=bf: e.matmul(
                PB[1], xfm_s(bf, dc, 128, 128),
                mm_v[:, dc * 128:(dc + 1) * 128], start=(dc == 0),
                stop=(dc == DCH - 1)), inc="sT")
        M(f"scP,{t},1", tp_)
        p.wait("vector", "sT", te0)
        p.selfw("vector")
        ve0 = p.op("vector", lambda e: e.tensor_copy(
            e0s_v[:], EB[0]), inc="sV")
        p.op("vector", lambda e, vsl=vsl: e.tensor_tensor(
            ys_v[:, 0:128], EB[0], vsl[:, 0:128],
            op=ALU.add), inc="sV")
        p.wait("tensor", "sV", ve0)
        tc1 = p.op("tensor", lambda e, bf=bf: e.matmul(
            PB[1], gts_s(bf, 2), e0s_v[:], start=False,
            stop=True, skip_group_check=True), inc="sT")
        v_group(t + 1, 0)
        p.wait("vector", "sT", tc1)
        vp1 = p.op("vector", lambda e: e.tensor_tensor(
            p1s_v, PB[1], T["bvbc"][:], op=ALU.subtract),
            inc="sV")
        M(f"scPe,{t}", vp1)
        p.wait("tensor", "sV", vp1)
        te1 = p.op("tensor", lambda e, bf=bf: e.matmul(
            EB[1], gts_s(bf, 1), p1s_v, start=True,
            stop=True), inc="sT")
        M(f"scE1,{t}", te1)
        v_group(t + 1, 1)
        xtm_half(t + 1, 1)
        p.wait("vector", "sT", te1)
        p.selfw("vector")
        ve1 = p.op("vector", lambda e: e.tensor_copy(
            e1s_v[:], EB[1]), inc="sV")
        vy1 = p.op("vector", lambda e, vsl=vsl: e.tensor_tensor(
            ys_v[:, 128:256], EB[1], vsl[:, 128:256],
            op=ALU.add), inc="sV")
        M(f"scEe,{t}", ve1)
        p.wait("sync", "sV", vy1)
        p.dma("sync", dr["ycon"].ap()[t * CH:t * CH + 128, :],
              ys_v[:, 0:128], "qYS")
        yq = p.dma("sync", dr["ycon"].ap()[t * CH + 128:(t + 1) * CH, :],
                   ys_v[:, 128:256], "qYS")
        M(f"scYq,{t}", yq)
        if t + 1 >= NCHUNK:
            return
        p.wait("tensor", "sV", ve1)
        p.wait("tensor", "sA", M(f"xtme,{t}"))
        tu = {}
        for half in range(2):
            for tt in range(2):
                es = e0s_v if tt == 0 else e1s_v
                for d4 in range(4):
                    dc = half * 4 + d4
                    tu[half] = p.op(
                        "tensor", lambda e, tt=tt, dc=dc, d4=d4, es=es,
                        bf=bf, half=half: e.matmul(
                            UBK[half][:, d4 * 128:(d4 + 1) * 128],
                            xtm_s(bf, tt, dc * 128, 128),
                            es[:], start=(tt == 0 and d4 == 0),
                            stop=(tt == 1 and d4 == 3)),
                        inc="sT")
        M(f"scU,{t}", tu[1])
        vm = {}
        for half in range(2):
            p.wait("vector", "sT", tu[half])
            if half == 0:
                p.selfw("vector")
            vm[half] = p.op("vector", lambda e, half=half: e.tensor_tensor(
                mm_v[:, half * 512:(half + 1) * 512],
                mm_v[:, half * 512:(half + 1) * 512],
                UBK[half][:], op=ALU.add), inc="sV")
        M(f"scM,{t},0", vm[0])
        M(f"scM,{t}", vm[1])

    xtm_half(0, 0)
    xtm_half(0, 1)
    v_group(0, 0)
    v_group(0, 1)
    for t in range(NCHUNK):
        scan_chunk(t)
        if t + 2 < NCHUNK:
            chunk_loads(t + 2)

    p.wait("gpsimd", "qYS", p.cnt("qYS"))
    p.op("gpsimd", lambda e: e.collective_compute(
        "AllGather", ALU.bypass, replica_groups=[list(range(NCORE))],
        ins=[dr["ycon"].ap().opt()], outs=[dr["yg"].ap().opt()]), inc="cc")

    for eng in ("tensor", "vector", "scalar", "sync"):
        p.wait(eng, "cc", 4)
    p.wait("sync", "qXP", p.cnt("qXP"))
    p.wait("sync", "qXF", p.cnt("qXF"))
    p.wait("sync", "sA", p.cnt("sA"))
    rl = p.dma("sync", T["wbc"][:], dr["wbc"].ap(), "qInit")
    p.wait("vector", "qInit", rl)
    p.wait("scalar", "qInit", rl)
    p.wait("tensor", "qInit", rl)
    ffn_head2_loads(0)
    ffn_head2_loads(1)
    ffn_head(0, 2)
    ffn_body_ut(0, 2)
    for i in range(NT):
        if i + 2 < NT:
            ffn_head2_loads(i + 2)
        nx = (lambda i=i: ffn_head(i + 1, 2)) if i + 1 < NT else None
        ffn_body(i, 2, nxt=nx)

    if debug:
        p.dma("gpsimd", dr["agcp"].ap(), dr["agin"].ap(), "qDC")
        p.wait("gpsimd", "qDC", p.cnt("qDC"))
        p.dma("gpsimd", dr["yccp"].ap(), dr["ycon"].ap(), "qDC")
        p.wait("gpsimd", "qDC", p.cnt("qDC"))
    for q in ("qSX0", "qSX1", "qGP", "qX1S"):
        p.wait(eng="gpsimd", semname=q, val=p.cnt(q))

    with nc.Block() as block:

        @block.tensor
        def _(eng):
            for f in p.streams["tensor"]:
                f(eng, None)

        @block.vector
        def _(eng):
            for f in p.streams["vector"]:
                f(eng, None)

        @block.scalar
        def _(eng):
            for f in p.streams["scalar"]:
                f(eng, None)

        @block.sync
        def _(eng):
            for f in p.streams["sync"]:
                f(eng, None)

        @block.gpsimd
        def _(eng):
            with eng.register("ry") as ry:
                regs = {"ry": ry}
                for f in p.streams["gpsimd"]:
                    f(eng, regs)

    stack.close()
    return nc


def _host_inputs(L0_attn_x, L0_attn_W0, L0_attn_Wkv, L0_attn_bkv, L0_attn_eta,
                 L0_ffn_W_fc, L0_ffn_ln_w, L0_ffn_ln_b, L0_ffn_W_proj):
    f32 = np.float32
    xr_full = np.ascontiguousarray(
        np.transpose(np.asarray(L0_attn_x, f32), (1, 0, 2)).reshape(NTOK, D))
    wfcT = np.ascontiguousarray(np.asarray(L0_ffn_W_fc, f32).T)
    wpT = np.ascontiguousarray(np.asarray(L0_ffn_W_proj, f32).T)
    W0 = np.asarray(L0_attn_W0, f32)
    Wkv = np.asarray(L0_attn_Wkv, f32)
    bkv = np.asarray(L0_attn_bkv, f32)
    ln_w = np.asarray(L0_ffn_ln_w, f32)
    ln_b = np.asarray(L0_ffn_ln_b, f32)
    eta = np.asarray(L0_attn_eta, f32).reshape(1, 1)

    idf = np.eye(128, dtype=f32)
    idb = idf.astype(ml_dtypes.bfloat16)
    ii = np.arange(128)
    maskl = -((ii[:, None] // 2) > (ii[None, :] // 2)).astype(f32)
    masku = np.ascontiguousarray(maskl.T)
    consts = np.zeros((128, 4), f32)
    consts[:, 0] = EPS_LN
    ones2 = np.ones((2, 128), f32).astype(ml_dtypes.bfloat16)
    wbc = np.ascontiguousarray(np.broadcast_to(ln_w[None, :], (128, H)))
    bcol = np.ascontiguousarray(ln_b.reshape(HCH, 128).T)
    wcol = np.ascontiguousarray(ln_w.reshape(HCH, 128).T)

    in_maps = []
    for k in range(NCORE):
        cols = slice(128 * k, 128 * (k + 1))
        yoff = np.zeros((NT, 1), np.uint32)
        for i in range(NT):
            yoff[i, 0] = (k * TPC + i * 128) * 128
        wslab = np.concatenate([
            wfcT[128 * k:128 * (k + 1), :],
            wpT[512 * k:512 * (k + 1), :].reshape(4, 128, D).transpose(
                1, 0, 2).reshape(128, H)]).astype(np.float32)
        in_maps.append({
            "xr": xr_full[k * TPC:(k + 1) * TPC],
            "wfcT": wfcT, "wpT": wpT,
            "w0T": np.ascontiguousarray(W0[cols, :].T),
            "wvT": np.ascontiguousarray(Wkv[D:2 * D][cols, :].T),
            "bvbc": np.ascontiguousarray(
                np.broadcast_to(bkv[D:2 * D][cols][None, :], (128, 128))),
            "wbc": wbc, "bcol": bcol, "wcol": wcol, "eta": eta,
            "idf": idf, "idb": idb, "maskl": maskl, "masku": masku,
            "consts": consts, "yoff": yoff, "wslab": np.ascontiguousarray(wslab),
        })
    return in_maps


def kernel(L0_attn_x, L0_attn_W0, L0_attn_Wkv, L0_attn_bkv, L0_attn_eta,
           L0_ffn_W_fc, L0_ffn_ln_w, L0_ffn_ln_b, L0_ffn_W_proj):
    in_maps = _host_inputs(L0_attn_x, L0_attn_W0, L0_attn_Wkv, L0_attn_bkv,
                           L0_attn_eta, L0_ffn_W_fc, L0_ffn_ln_w, L0_ffn_ln_b,
                           L0_ffn_W_proj)
    nc = build_program()
    res = run_bass_kernel_spmd(nc, in_maps, core_ids=list(range(NCORE)))
    outs = [np.asarray(res.results[k]["out"]) for k in range(NCORE)]
    full = np.concatenate(outs, axis=0)
    return np.ascontiguousarray(
        full.reshape(S, B, D).transpose(1, 0, 2)).astype(np.float32)
